# version 20
# baseline (speedup 1.0000x reference)
import numpy as np
import concourse.bass as bass
import concourse.mybir as mybir
from concourse.bass_utils import run_bass_kernel_spmd
from contextlib import ExitStack

F32 = mybir.dt.float32
BF16 = mybir.dt.bfloat16
AF = mybir.ActivationFunctionType
ALU = mybir.AluOpType
AX = mybir.AxisListType

D = 1024
NCH = 8
SEQ = 8192
NBATCH = 4
NCORE = 8
GQ = 256
NG = 16
KT = 512
EPS = 1e-5
SCALE = 1.0 / 32.0
EXPB = -64.0
RING = 4
PTRING = 8
RACE_DETECT = False
TRUNC = None
FENCE_MODE = "full"

ENGINES = ("pe", "act", "dve", "sync", "pool")
ENG_SEM = {"pe": "pe", "act": "act", "dve": "dve", "sync": "msync",
           "pool": "mpool"}
DRAIN_ENGINES = {"act", "dve"}


class Plan:

    def __init__(self, needs=None):
        self.needs = needs
        self.found = set()
        self.ops = {e: [] for e in ENGINES}
        self.cnt = {}
        self.lastwait = {}
        self.seq = {e: 0 for e in ENGINES}
        self.handle = {}
        self.res = {}

    def op(self, eng, fn, reads=(), writes=(), inc=None, inc_by=1,
           waits=(), dma=False, drain=False):
        seq = self.seq[eng]
        self.seq[eng] += 1
        deps = list(waits)
        need_drain = drain

        def want(key):
            h = self.handle.get(key)
            if h is None:
                self.found.add(key)
            else:
                deps.append(h)

        for key in list(reads) + list(writes):
            st = self.res.get(key)
            if st is None:
                continue
            w = st["w"]
            if w is not None:
                if w[0] == eng:
                    if eng in DRAIN_ENGINES:
                        need_drain = True
                else:
                    want(w)
            if key in writes:
                for reng, rseq in st["r"].items():
                    if reng == eng:
                        if eng in DRAIN_ENGINES:
                            need_drain = True
                    else:
                        want((reng, rseq))
        if (self.needs is not None and inc is None
                and (eng, seq) in self.needs):
            inc = ENG_SEM[eng]
            if eng in DRAIN_ENGINES:
                need_drain = True
        val = None
        if inc is not None:
            self.cnt[inc] = self.cnt.get(inc, 0) + inc_by
            val = self.cnt[inc]
            self.handle[(eng, seq)] = (inc, val)
        else:
            self.handle[(eng, seq)] = None
        for key in reads:
            st = self.res.setdefault(key, {"w": None, "r": {}})
            st["r"][eng] = seq
        for key in writes:
            st = self.res.setdefault(key, {"w": None, "r": {}})
            st["w"] = (eng, seq)
            st["r"] = {}
        real = []
        for s, v in deps:
            if v is None or v <= 0:
                continue
            k = (eng, s)
            if self.lastwait.get(k, 0) >= v:
                continue
            self.lastwait[k] = v
            real.append((s, v))
        self.ops[eng].append((real, need_drain, fn, inc, inc_by))
        return val

    def fence(self, qwaits=None):
        qw = list(qwaits or ())
        if FENCE_MODE == "none":
            self.res = {}
            return
        for e in ("pe", "act", "dve"):
            if e == "dve":
                fn = lambda x: x.engine_nop()
                w = []
            elif e == "act":
                fn = lambda x: x.activation(out=self.junk[0:1, 0:1],
                                            in_=self.junk[0:1, 1:2],
                                            func=AF.Copy)
                w = ["junkt"]
            else:
                fn = lambda x: x.matmul(out=self.pejunk[0:1, 0:1],
                                        lhsT=self.peones[0:1, 0:1],
                                        rhs=self.peones[0:1, 0:1],
                                        start=True, stop=True)
                w = ["psX"]
            self.op(e, fn, writes=[f"_f1{e}"] + w,
                    reads=(["onescol"] if e == "pe" else []),
                    drain=(e in DRAIN_ENGINES))
        if FENCE_MODE == "f1":
            self.res = {}
            return
        for e in ENGINES:
            self.op(e, lambda x: x.nop(),
                    reads=["_f1pe", "_f1act", "_f1dve"], waits=qw)
        self.res = {}

    def emit(self, block, sems):
        def mk(name):
            def run(eng):
                for waits, drain, fn, inc, inc_by in self.ops[name]:
                    for s, v in waits:
                        eng.wait_ge(sems[s], v)
                    if drain:
                        eng.drain()
                    inst = fn(eng)
                    if inc is not None:
                        inst.then_inc(sems[inc], inc_by)
            return run
        block.tensor(mk("pe"))
        block.scalar(mk("act"))
        block.vector(mk("dve"))
        block.sync(mk("sync"))
        block.gpsimd(mk("pool"))


SEM_NAMES = (["pe", "act", "dve", "msync", "mpool", "dW", "dOUT"]
             + [f"q_kt{i}" for i in range(RING)]
             + [f"q_vt{i}" for i in range(RING)]
             + ["q_qt0", "q_qt1", "q_vo0", "q_vo1", "q_x0g0", "q_x0g1",
                "q_xs0", "q_xs1", "q_xsbf0", "q_xsbf1",
                "q_kstg", "q_vstg",
                "dD_ov", "dD_xov", "dD_a", "dD_x1", "dD_f"])


def _plan(P, ng, n_slab_a, dump, p0_only=False):
    nc = bass.Bass("TRN2", target_bir_lowering=False,
               detect_race_conditions=RACE_DETECT)
    scrkind = "ExternalOutput" if dump else "Internal"
    n_slab_b = (ng * GQ) // KT

    dr = {}
    def din(name, shape, dt=F32):
        dr[name] = nc.dram_tensor(name, shape, dt, kind="ExternalInput").ap()
    din("xkvC", [SEQ // KT, 128, NCH * KT])
    din("xqC", [NG * GQ // KT, 128, NCH * KT])
    din("xq", [NG * GQ, D])
    din("wq", [D, D]); din("wk", [D, D]); din("wv", [D, D]); din("wfc", [D, D])
    din("bfc", [1, D])
    din("gT", [128, NCH])
    din("bT", [128, NCH])
    din("basewt", [D, D])
    din("baseb", [1, D])
    din("sat", [D, 16])
    din("sbt", [16, D])
    din("rat", [D, 96])
    din("rbt", [96, D])
    din("rwt", [D, 6])
    din("rb", [1, 6])
    din("rep", [6, 96])
    din("ident", [128, 128])
    din("maskT", [KT, GQ])

    out = nc.dram_tensor("out", [NG, 2, 128, D], F32,
                         kind="ExternalOutput").ap()
    dbg = {}
    if dump:
        for nm in ("dbg_ov", "dbg_x1", "dbg_f", "dbg_xov", "dbg_a"):
            dbg[nm] = nc.dram_tensor(nm, [NG, 128, 2 * D], F32,
                                     kind="ExternalOutput").ap()

    kscr = nc.dram_tensor("kscr", [SEQ // KT, 128, NCH * KT], BF16,
                          kind=scrkind).ap()
    vscr = nc.dram_tensor("vscr", [SEQ // KT, 128, 4 * D], BF16,
                          kind=scrkind).ap()
    qscr = nc.dram_tensor("qscr", [NG, 128, NCH * GQ], BF16,
                          kind=scrkind).ap()
    with ExitStack() as st:
        def sb(name, shape, dt):
            return st.enter_context(nc.sbuf_tensor("s_" + name, shape, dt))
        def psal(name, shape, dt=F32):
            return st.enter_context(nc.psum_tensor(name, shape, dt))

        psS = [psal("psS0", [128, 512]), psal("psS1", [128, 512])]
        psBig = psal("psBig", [128, 2048])
        psX = psal("psX", [128, 512])
        psT = psal("psT", [128, 1024], BF16)
        BIGK = ["psB0", "psB1", "psB2", "psB3"]

        wfc = sb("wfc", [128, NCH * D], BF16)
        bwt = sb("bwt", [128, NCH * D], BF16)
        satb = sb("satb", [128, NCH * 16], BF16)
        sbtb = sb("sbtb", [16, D], BF16)
        ratb = sb("ratb", [128, NCH * 96], BF16)
        rbtb = sb("rbtb", [96, D], BF16)
        rwtb = sb("rwtb", [128, NCH * 6], BF16)
        rbrow = sb("rbrow", [1, 6], BF16)
        repb = sb("repb", [6, 96], BF16)
        identb = sb("identb", [128, 128], BF16)
        maskt = sb("maskt", [128, 4 * GQ], BF16)
        gamT = sb("gamT", [128, NCH], F32)
        betT = sb("betT", [128, NCH], F32)
        bt8 = sb("bt8", [128, NCH], BF16)
        bfcrow = sb("bfcrow", [1, D], F32)
        bbrow = sb("bbrow", [1, D], BF16)
        ccbf = sb("ccbf", [1, D], BF16)
        onesf = sb("onesf", [1, 128], F32)
        ones1p = sb("ones1p", [1, 128], BF16)
        onescol = sb("onescol", [128, 1], BF16)
        cst = sb("cst", [128, 2], F32)
        junk = sb("junk", [1, 2], F32)
        ccb_bc = sb("ccb_bc", [128, D], BF16)
        bb_bc = sb("bb_bc", [128, D], BF16)
        P.junk = junk
        P.peones = onescol
        P.pejunk = psX

        sems = {n: st.enter_context(nc.semaphore(n)) for n in SEM_NAMES}

        def MM(o, l, r, start, stop):
            return lambda t: t.matmul(out=o, lhsT=l, rhs=r,
                                      start=start, stop=stop)

        P.op("dve", lambda e: e.memset(onesf[:], 1.0), writes=["onesf"])
        P.op("dve", lambda e: e.memset(ones1p[:], 1.0), writes=["ones1p"])
        P.op("dve", lambda e: e.memset(onescol[:], 1.0), writes=["onescol"])
        P.op("dve", lambda e: e.memset(cst[:, 0:1], EPS), writes=["cst"])
        P.op("dve", lambda e: e.memset(junk[:], 0.0), writes=["junkt"])
        P.op("dve", lambda e: e.memset(cst[:, 1:2], EXPB), writes=["cst"])

        def WL(dst, src, wkey):
            P.op("pool", lambda e, d=dst, s=src: e.dma_start(out=d, in_=s),
                 writes=[wkey], inc="dW", inc_by=16, dma=True)

        def rr(a):
            return a.rearrange("(c p) o -> p c o", p=128)

        def chv(t, x):
            return t[:].rearrange("p (c o) -> p c o", o=x)

        epiw = [
            lambda: WL(chv(wfc, D), rr(dr["wfc"]), "wfc"),
            lambda: WL(chv(bwt, D), rr(dr["basewt"]), "bwt"),
            lambda: WL(chv(satb, 16), rr(dr["sat"]), "satb"),
            lambda: WL(sbtb[:], dr["sbt"], "sbtb"),
            lambda: WL(chv(ratb, 96), rr(dr["rat"]), "ratb"),
            lambda: WL(rbtb[:], dr["rbt"], "rbtb"),
            lambda: WL(chv(rwtb, 6), rr(dr["rwt"]), "rwtb"),
            lambda: WL(rbrow[:], dr["rb"], "rbrow"),
            lambda: WL(repb[:], dr["rep"], "repb"),
            lambda: WL(identb[:], dr["ident"], "identb"),
            lambda: WL(maskt[:].rearrange("p (s q) -> p s q", q=GQ),
                       dr["maskT"].rearrange("(s p) q -> p s q", p=128),
                       "maskt"),
            lambda: WL(gamT[:], dr["gT"], "gamT"),
            lambda: WL(betT[:], dr["bT"], "betT"),
            lambda: WL(bfcrow[:], dr["bfc"], "bfcrow"),
            lambda: WL(bbrow[:], dr["baseb"], "bbrow"),
        ]

        def plan_epi_fence():
            P.op("pool", lambda e: e.nop(),
                 waits=[("dW", P.cnt["dW"])],
                 writes=["wfc", "bwt", "satb", "sbtb", "ratb", "rbtb",
                         "rwtb", "rbrow", "repb", "identb", "maskt", "gamT",
                         "betT", "bfcrow", "bbrow"])

        def plan_cconst():
            P.op("dve", lambda e: e.tensor_copy(out=bt8[:], in_=betT[:]),
                 reads=["betT"], writes=["bt8"])
            for h in range(2):
                for i in range(NCH):
                    P.op("pe", MM(
                        psX[0:1, 0:512], bt8[:, i:i + 1],
                        wfc[:, i * D + h * 512: i * D + (h + 1) * 512],
                        i == 0, i == NCH - 1),
                        reads=["bt8", "wfc"], writes=["psX"])
                P.op("dve", lambda e, h=h: e.tensor_add(
                    out=ccbf[0:1, h * 512:(h + 1) * 512],
                    in0=psX[0:1, 0:512],
                    in1=bfcrow[0:1, h * 512:(h + 1) * 512]),
                    reads=["psX", "bfcrow"], writes=["ccbf"])
            for h in range(2):
                P.op("pe", MM(psX[:, 0:512], ones1p[:],
                              ccbf[0:1, h * 512:(h + 1) * 512], True, True),
                     reads=["ones1p", "ccbf"], writes=["psX"])
                P.op("dve", lambda e, h=h: e.tensor_copy(
                    out=ccb_bc[:, h * 512:(h + 1) * 512], in_=psX[:, 0:512]),
                    reads=["psX"], writes=["ccb_bc"])
                P.op("pe", MM(psX[:, 0:512], ones1p[:],
                              bbrow[0:1, h * 512:(h + 1) * 512], True, True),
                     reads=["ones1p", "bbrow"], writes=["psX"])
                P.op("dve", lambda e, h=h: e.tensor_copy(
                    out=bb_bc[:, h * 512:(h + 1) * 512], in_=psX[:, 0:512]),
                    reads=["psX"], writes=["bb_bc"])
            for i in range(NCH):
                P.op("dve", lambda e, i=i: e.tensor_scalar_mul(
                    out=wfc[:, i * D:(i + 1) * D],
                    in0=wfc[:, i * D:(i + 1) * D],
                    scalar1=gamT[:, i:i + 1]),
                    reads=["wfc", "gamT"], writes=["wfc"])

        with ExitStack() as st0:
            def sb0(name, shape, dt):
                return st0.enter_context(
                    nc.sbuf_tensor("s_" + name, shape, dt))
            wq = sb0("wq", [128, NCH * D], BF16)
            wk = sb0("wk", [128, NCH * D], BF16)
            wv = sb0("wv", [128, NCH * D], BF16)
            xs32 = sb0("xs32", [128, 2 * NCH * 512], F32)
            xsbf = sb0("xsbf", [128, NCH * 512], BF16)
            xsq = sb0("xsq", [128, NCH * 512], BF16)
            xhat = sb0("xhat", [128, 2 * NCH * 512], BF16)
            kstg = sb0("kstg", [128, NCH * 512], BF16)
            vstg = sb0("vstg", [128, 4 * D], BF16)
            r_mu = sb0("r_mu", [1, 512], F32)
            r_e2 = sb0("r_e2", [1, 512], F32)
            r_mu2 = sb0("r_mu2", [1, 512], F32)
            r_var = sb0("r_var", [1, 512], F32)
            r_r = sb0("r_r", [1, 512], F32)
            r_mr = sb0("r_mr", [1, 512], F32)
            hold_r = sb0("hold_r", [1, 512], F32)
            hold_m = sb0("hold_m", [1, 512], F32)
            xtmp = sb0("xtmp", [128, 512], F32)

            def p0_dma_xs(sl):
                srcC, tok_off, kdst, vdst, is_q, kdkeys, vdkeys, idx = sl
                buf = idx % 2
                xo = buf * NCH * 512
                src_ = srcC[tok_off // KT]
                P.op("sync", lambda e: e.dma_start(
                    out=xs32[:, xo:xo + NCH * 512], in_=src_),
                    writes=[f"xs{buf}"], inc=f"q_xs{buf}", inc_by=16,
                    dma=True)

            def p0_dma_xsbf(sl):
                srcC, tok_off, kdst, vdst, is_q, kdkeys, vdkeys, idx = sl
                if is_q:
                    return
                buf = idx % 2
                src_ = srcC[tok_off // KT]
                P.op("pool", lambda e: e.dma_start(
                    out=xsbf[:], in_=src_),
                    writes=["xsbf"], inc=f"q_xsbf{buf}", inc_by=16,
                    dma=True)

            def p0_stats_mm(sl):
                srcT, tok_off, kdst, vdst, is_q, kdkeys, vdkeys, idx = sl
                if is_q:
                    return
                buf = idx % 2
                xo = buf * NCH * 512
                xq_ = xsq
                P.op("act", lambda e: e.activation(
                    out=xq_[:], in_=xs32[:, xo:xo + NCH * 512],
                    func=AF.Square), reads=[f"xs{buf}"],
                    writes=["xsq"])
                for i in range(NCH):
                    P.op("pe", MM(psX[0:1, 0:512], onescol[:],
                                  xsbf[:, i * 512:(i + 1) * 512],
                                  i == 0, i == NCH - 1),
                         reads=["onescol", "xsbf"], writes=["psX"])
                for i in range(NCH):
                    P.op("pe", MM(psS[0][0:1, 0:512], onescol[:],
                                  xq_[:, i * 512:(i + 1) * 512],
                                  i == 0, i == NCH - 1),
                         reads=["onescol", "xsq"], writes=["psS0"])

            def p0_rows(sl):
                srcT, tok_off, kdst, vdst, is_q, kdkeys, vdkeys, idx = sl
                buf = idx % 2
                if is_q:
                    P.op("dve", lambda e: e.tensor_copy(
                        out=r_r[0:1, 0:512], in_=hold_r[0:1, 0:512]),
                        reads=["hold_r0", "hold_r1"], writes=["r_r"])
                    P.op("dve", lambda e: e.tensor_copy(
                        out=r_mr[0:1, 0:512], in_=hold_m[0:1, 0:512]),
                        reads=["hold_m0", "hold_m1"], writes=["r_mr"])
                    return
                rmu = r_mu[0:1, 0:512]
                rvv = r_var[0:1, 0:512]
                rrr = r_r[0:1, 0:512]
                P.op("dve", lambda e: e.tensor_scalar_mul(
                    out=rmu, in0=psX[0:1, 0:512], scalar1=1.0 / D),
                    reads=["psX"], writes=["r_mu"])
                P.op("dve", lambda e: e.tensor_scalar_mul(
                    out=r_e2[0:1, 0:512], in0=psS[0][0:1, 0:512],
                    scalar1=1.0 / D),
                    reads=["psS0"], writes=["r_e2"])
                P.op("act", lambda e: e.activation(
                    out=r_mu2[0:1, 0:512], in_=rmu, func=AF.Square),
                    reads=["r_mu"], writes=["r_mu2"])
                P.op("dve", lambda e: e.tensor_sub(
                    out=rvv, in0=r_e2[0:1, 0:512], in1=r_mu2[0:1, 0:512]),
                    reads=["r_e2", "r_mu2"], writes=["r_var"])
                P.op("act", lambda e: e.activation(
                    out=rvv, in_=rvv, func=AF.Ln, bias=cst[0:1, 0:1]),
                    reads=["r_var", "cst"], writes=["r_var"])
                P.op("act", lambda e: e.activation(
                    out=rrr, in_=rvv, func=AF.Exp, scale=-0.5),
                    reads=["r_var"], writes=["r_r"])
                P.op("dve", lambda e: e.tensor_mul(
                    out=r_mr[0:1, 0:512], in0=rmu, in1=rrr),
                    reads=["r_mu", "r_r"], writes=["r_mr"])
                hp = (tok_off // KT) % 2
                P.op("dve", lambda e, hp=hp: e.tensor_copy(
                    out=hold_r[0:1, hp * 256:(hp + 1) * 256],
                    in_=rrr[0:1, 0:256]),
                    reads=["r_r"], writes=[f"hold_r{hp}"])
                P.op("dve", lambda e, hp=hp: e.tensor_copy(
                    out=hold_m[0:1, hp * 256:(hp + 1) * 256],
                    in_=r_mr[0:1, 0:256]),
                    reads=["r_mr"], writes=[f"hold_m{hp}"])

            def p0_xhat(sl):
                srcT, tok_off, kdst, vdst, is_q, kdkeys, vdkeys, idx = sl
                buf = idx % 2
                xo = buf * NCH * 512
                rrr = r_r[0:1, 0:512]
                P.op("pe", MM(psS[1][:, 0:512], onesf[:], rrr, True, True),
                     reads=["onesf", "r_r"], writes=["psS1"])
                P.op("pe", MM(psS[0][:, 0:512], onesf[:], r_mr[0:1, 0:512],
                              True, True),
                     reads=["onesf", "r_mr"], writes=["psS0"])
                for i in range(NCH):
                    P.op("dve", lambda e, i=i: e.tensor_mul(
                        out=xtmp[:],
                        in0=xs32[:, xo + i * 512: xo + (i + 1) * 512],
                        in1=psS[1][:, 0:512]),
                        reads=[f"xs{buf}", "psS1"],
                        writes=["xtmp"])
                    P.op("dve", lambda e, i=i: e.tensor_sub(
                        out=xhat[:, buf * NCH * 512 + i * 512:
                                 buf * NCH * 512 + (i + 1) * 512],
                        in0=xtmp[:],
                        in1=psS[0][:, 0:512]),
                        reads=["xtmp", "psS0"],
                        writes=[f"xhat{buf}_{i}"])

            def p0_proj(sl):
                srcT, tok_off, kdst, vdst, is_q, kdkeys, vdkeys, idx = sl
                buf = idx % 2
                xh = lambda i: xhat[:, buf * NCH * 512 + i * 512:
                                    buf * NCH * 512 + (i + 1) * 512]
                wten = wq if is_q else wk
                wkey = "wq" if is_q else "wk"
                for o in range(NCH):
                    bank = BIGK[o % 4]
                    for i in range(NCH):
                        P.op("pe", MM(
                            psBig[:, (o % 4) * 512:(o % 4 + 1) * 512],
                            wten[:, i * D + o * 128: i * D + (o + 1) * 128],
                            xh(i), i == 0, i == NCH - 1),
                            reads=[wkey, f"xhat{buf}_{i}"], writes=[bank])
                    P.op("dve", lambda e, o=o: e.tensor_copy(
                        out=kstg[:, o * 512:(o + 1) * 512],
                        in_=psBig[:, (o % 4) * 512:(o % 4 + 1) * 512]),
                        reads=[bank], writes=["kstg"])
                if is_q:
                    t2 = tok_off // KT
                    for u in range(2):
                        P.op("sync", lambda e, u=u, t2=t2: e.dma_start(
                            out=qscr[2 * t2 + u].rearrange(
                                "p (c q) -> p c q", q=GQ),
                            in_=kstg[:].rearrange(
                                "p (c t) -> p c t", t=512)[:, :, u * GQ:
                                                           (u + 1) * GQ]),
                            reads=["kstg"], writes=[kdkeys[u]],
                            inc="q_kstg", inc_by=16, dma=True)
                else:
                    P.op("sync", lambda e, kdst=kdst: e.dma_start(
                        out=kdst, in_=kstg[:]),
                        reads=["kstg"], writes=list(kdkeys),
                        inc="q_kstg", inc_by=16, dma=True)
                if is_q:
                    return
                for s4 in range(4):
                    for h in range(2):
                        bank = BIGK[(s4 * 2 + h) % 4]
                        bs = ((s4 * 2 + h) % 4) * 512
                        for i in range(NCH):
                            P.op("pe", MM(
                                psBig[:, bs:bs + 512],
                                xh(i)[:, s4 * 128: s4 * 128 + 128],
                                wv[:, i * D + h * 512: i * D + (h + 1) * 512],
                                i == 0, i == NCH - 1),
                                reads=[f"xhat{buf}_{i}", "wv"],
                                writes=[bank])
                        P.op("dve", lambda e, s4=s4, h=h, bs=bs:
                             e.tensor_copy(
                            out=vstg[:, s4 * D + h * 512:
                                     s4 * D + (h + 1) * 512],
                            in_=psBig[:, bs:bs + 512]),
                            reads=[bank], writes=["vstg"])
                P.op("sync", lambda e, vdst=vdst: e.dma_start(
                    out=vdst, in_=vstg[:]),
                    reads=["vstg"], writes=list(vdkeys),
                    inc="q_vstg", inc_by=16, dma=True)

            order = []
            for s in range(n_slab_b):
                order += [("kv", 2 * s), ("kv", 2 * s + 1), ("q", s)]
            for t in range(2 * n_slab_b, n_slab_a):
                order.append(("kv", t))
            slabs = []
            for idx, (kind, t) in enumerate(order):
                if kind == "kv":
                    slabs.append((dr["xkvC"], t * KT, kscr[t], vscr[t],
                                  False, [f"kscr{t}"], [f"vscr{t}"], idx))
                else:
                    slabs.append((dr["xqC"], t * KT, None, None, True,
                                  [f"qscr{2 * t}", f"qscr{2 * t + 1}"],
                                  [], idx))
            nsl = len(slabs)
            if nsl:
                p0_dma_xs(slabs[0])
                p0_dma_xsbf(slabs[0])
                if nsl > 1:
                    p0_dma_xs(slabs[1])
                WL(chv(wk, D), rr(dr["wk"]), "wk")
                WL(chv(wv, D), rr(dr["wv"]), "wv")
                WL(chv(wq, D), rr(dr["wq"]), "wq")
                P.op("pool", lambda e: e.nop(),
                     waits=[("dW", P.cnt["dW"])], writes=["wq", "wk", "wv"])
                p0_stats_mm(slabs[0])
                p0_rows(slabs[0])
                p0_xhat(slabs[0])
                for t in range(nsl):
                    if t + 1 < nsl:
                        p0_dma_xsbf(slabs[t + 1])
                        p0_stats_mm(slabs[t + 1])
                        p0_rows(slabs[t + 1])
                    if t + 2 < nsl:
                        p0_dma_xs(slabs[t + 2])
                    p0_proj(slabs[t])
                    if t >= 1 and epiw:
                        epiw.pop(0)()
                        if len(epiw) >= 10:
                            epiw.pop(0)()
                    if t + 1 < nsl:
                        p0_xhat(slabs[t + 1])
                while epiw:
                    epiw.pop(0)()
                plan_epi_fence()
                plan_cconst()

        P.fence(qwaits=[(s, P.cnt.get(s, 0)) for s in
                        ("q_kstg", "q_vstg", "q_xsbf0", "q_xsbf1", "dW",
                         "q_xs0", "q_xs1")])

        ktile = sb("ktile", [128, RING * NCH * KT], BF16)
        vtile = sb("vtile", [128, RING * 4 * D], BF16)
        qtile = sb("qtile", [128, 2 * NCH * GQ], BF16)
        vown = sb("vown", [128, 2 * 2 * D], BF16)
        x0g = sb("x0g", [128, 2 * 2 * D], F32)
        ptb = sb("ptb", [128, PTRING * GQ], BF16)
        ovbf = sb("ovbf", [128, 2 * 2 * D], BF16)
        xov = sb("xov", [128, 2 * D], BF16)
        xovT = sb("xovT", [128, NCH * GQ], BF16)
        sqbuf = sb("sqbuf", [128, D], BF16)
        astg = sb("astg", [128, 2 * D], F32)
        tscr = sb("tscr", [128, D], F32)
        x1buf = sb("x1buf", [128, 2 * D], F32)
        hs_sb = sb("hs_sb", [16, GQ], BF16)
        h96_sb = sb("h96_sb", [96, GQ], BF16)
        lg_sb = sb("lg_sb", [128, 12], F32)
        esb = sb("esb", [128, 12], F32)
        msk12 = sb("msk12", [128, 12], F32)
        wtm = sb("wtm", [128, 12], BF16)
        wfm = sb("wfm", [6, GQ], BF16)
        wh_sb = sb("wh_sb", [96, GQ], BF16)
        sm = sb("sm", [128, 32], F32)
        rsum = sb("rsum", [128, 4], F32)

        def kv_k(slot, i, sbi):
            off = slot * NCH * KT + i * KT + sbi * 128
            return ktile[:, off:off + 128]

        def kv_v(slot, sbi, h):
            off = slot * 4 * D + sbi * D + h * 512
            return vtile[:, off:off + 512]

        def q_ap(bufi, i):
            off = bufi * NCH * GQ + i * GQ
            return qtile[:, off:off + GQ]

        def pt_ap(m, qb=None):
            off = (m % PTRING) * GQ + (0 if qb is None else qb * 128)
            return ptb[:, off:off + (GQ if qb is None else 128)]

        mglob = [0]
        tcache = {}
        tclock = [0]

        def tile_slot(kb, future):
            for s, k in tcache.items():
                if k == kb:
                    return s
            if len(tcache) < RING:
                s = len(tcache)
            else:
                def nxt(k):
                    try:
                        return future.index(k)
                    except ValueError:
                        return 10 ** 9
                s = max(tcache, key=lambda sl: nxt(tcache[sl]))
            tcache[s] = kb
            P.op("sync", lambda e, kb=kb, s=s: e.dma_start(
                out=ktile[:, s * NCH * KT:(s + 1) * NCH * KT],
                in_=kscr[kb]),
                reads=[f"kscr{kb}"], writes=[f"kt{s}"],
                inc=f"q_kt{s}", inc_by=16, dma=True)
            P.op("sync", lambda e, kb=kb, s=s: e.dma_start(
                out=vtile[:, s * 4 * D:(s + 1) * 4 * D],
                in_=vscr[kb]),
                reads=[f"vscr{kb}"], writes=[f"vt{s}"],
                inc=f"q_vt{s}", inc_by=16, dma=True)
            return s

        def load_qv(g):
            if g >= ng:
                return
            gb = g % 2
            P.op("pool", lambda e, g=g, gb=gb: e.dma_start(
                out=qtile[:, gb * NCH * GQ:(gb + 1) * NCH * GQ],
                in_=qscr[g]),
                reads=[f"qscr{g}"], writes=[f"qt{gb}"],
                inc=f"q_qt{gb}", inc_by=16, dma=True)
            P.op("pool", lambda e, g=g, gb=gb: e.dma_start(
                out=vown[:, gb * 2 * D:(gb + 1) * 2 * D],
                in_=vscr[g][:, 0:2 * D]),
                reads=[f"vscr{g}"], writes=[f"vo{gb}"],
                inc=f"q_vo{gb}", inc_by=16, dma=True)

        def load_x0(g):
            if g >= ng:
                return
            gb = g % 2
            P.op("pool", lambda e, g=g, gb=gb: e.dma_start(
                out=x0g[:, gb * 2 * D:(gb + 1) * 2 * D].rearrange(
                    "p (s d) -> p s d", d=D),
                in_=dr["xq"][g * GQ:(g + 1) * GQ, :].rearrange(
                    "(s p) d -> p s d", p=128)),
                writes=[f"x0g{gb}"], inc=f"q_x0g{gb}", inc_by=16,
                dma=True)

        def ln_tm(src_ap, src_keys, dst_ap, dst_keys):
            P.op("dve", lambda e: e.bn_stats(
                out=sm[:, 0:6], in_=src_ap[:, 0:512]),
                reads=list(src_keys), writes=["sm_b0"])
            P.op("dve", lambda e: e.bn_stats(
                out=sm[:, 6:12], in_=src_ap[:, 512:1024]),
                reads=list(src_keys), writes=["sm_b1"])
            P.op("dve", lambda e: e.bn_aggr(
                out=sm[:, 12:14], in_=sm[:, 0:12]),
                reads=["sm_b0", "sm_b1"], writes=["sm_mv"])
            P.op("act", lambda e: e.activation(
                out=sm[:, 14:15], in_=sm[:, 13:14], func=AF.Ln,
                bias=cst[:, 0:1]),
                reads=["sm_mv", "cst"], writes=["sm_ln"])
            P.op("act", lambda e: e.activation(
                out=sm[:, 15:16], in_=sm[:, 14:15], func=AF.Exp, scale=-0.5),
                reads=["sm_ln"], writes=["sm_r"])
            P.op("dve", lambda e: e.tensor_scalar(
                out=dst_ap, in0=src_ap, scalar1=sm[:, 12:13],
                scalar2=sm[:, 15:16], op0=ALU.subtract, op1=ALU.mult),
                reads=list(src_keys) + ["sm_mv", "sm_r"],
                writes=list(dst_keys))

        def transpose8(src, src_keys, qb):
            for i in range(NCH):
                P.op("pe", lambda t, i=i: t.transpose(
                    out=psT[:, i * 128:(i + 1) * 128],
                    in_=src[:, i * 128:(i + 1) * 128], identity=identb[:]),
                    reads=list(src_keys) + ["identb"], writes=["psT"])
            for i in range(NCH):
                P.op("dve", lambda e, i=i, qb=qb: e.tensor_copy(
                    out=xovT[:, i * GQ + qb * 128: i * GQ + qb * 128 + 128],
                    in_=psT[:, i * 128:(i + 1) * 128]),
                    reads=["psT"], writes=[f"xovT{qb}"])

        load_qv(0)
        load_qv(1)
        load_x0(0)
        load_x0(1)

        def epi_gen(g, gb, M):
                for qb in range(2):
                    ln_tm(ovbf[:, gb * 2 * D + qb * D:
                               gb * 2 * D + (qb + 1) * D],
                          [f"ovbf{gb}_{qb}"],
                          xov[:, qb * D:(qb + 1) * D], [f"xov{qb}"])
                    yield
                    transpose8(xov[:, qb * D:(qb + 1) * D], [f"xov{qb}"], qb)
                if dump:
                    P.op("pool", lambda e, g=g: e.dma_start(
                        out=dbg["dbg_xov"][g], in_=xov[:]),
                        reads=["xov0", "xov1"], inc="dD_xov", inc_by=16,
                        dma=True)

                yield
                for qb in range(2):
                    for h in range(2):
                        bidx = 2 * qb + h
                        sbank = "psS0" if bidx % 2 == 0 else "psS1"
                        bank = psS[bidx % 2]
                        for i in range(NCH):
                            P.op("pe", MM(
                                bank[:, 0:512],
                                xovT[:, i * GQ + qb * 128:
                                     i * GQ + qb * 128 + 128],
                                wfc[:, i * D + h * 512: i * D + (h + 1) * 512],
                                i == 0, False),
                                reads=[f"xovT{qb}", "wfc"], writes=[sbank])
                        P.op("pe", MM(bank[:, 0:512], ones1p[:],
                                      ccbf[0:1, h * 512:(h + 1) * 512],
                                      False, True),
                             reads=["ones1p", "ccbf"], writes=[sbank])
                        P.op("act", lambda e, qb=qb, h=h, bank=bank: e.activation(
                            out=astg[:, qb * D + h * 512: qb * D + (h + 1) * 512],
                            in_=bank[:, 0:512], func=AF.Copy),
                            reads=[sbank], writes=[f"astg{qb}"])
                        yield
                if dump:
                    P.op("pool", lambda e, g=g: e.dma_start(
                        out=dbg["dbg_a"][g], in_=astg[:]),
                        reads=["astg0", "astg1"], inc="dD_a", inc_by=16,
                        dma=True)

                yield
                for qb in range(2):
                    ln_tm(astg[:, qb * D:(qb + 1) * D], [f"astg{qb}"],
                          tscr[:], ["tscr"])
                    P.op("dve", lambda e, qb=qb, gb=gb: e.tensor_add(
                        out=x1buf[:, qb * D:(qb + 1) * D], in0=tscr[:],
                        in1=x0g[:, gb * 2 * D + qb * D:
                                gb * 2 * D + (qb + 1) * D]),
                        reads=["tscr", f"x0g{gb}"], writes=[f"x1buf{qb}"])
                load_x0(g + 2)
                if dump:
                    P.op("pool", lambda e, g=g: e.dma_start(
                        out=dbg["dbg_x1"][g], in_=x1buf[:]),
                        reads=["x1buf0", "x1buf1"], inc="dD_x1", inc_by=16,
                        dma=True)

                yield
                for qb in range(2):
                    ln_tm(x1buf[:, qb * D:(qb + 1) * D], [f"x1buf{qb}"],
                          xov[:, qb * D:(qb + 1) * D], [f"xov{qb}"])
                    yield
                    transpose8(xov[:, qb * D:(qb + 1) * D], [f"xov{qb}"], qb)

                yield
                for i in range(NCH):
                    P.op("pe", MM(psS[0][0:96, 0:256],
                                  ratb[:, i * 96:(i + 1) * 96],
                                  xovT[:, i * GQ:(i + 1) * GQ],
                                  i == 0, i == NCH - 1),
                         reads=["ratb", "xovT0", "xovT1"], writes=["psS0"])
                for i in range(NCH):
                    P.op("pe", MM(psS[0][0:16, 256:512],
                                  satb[:, i * 16:(i + 1) * 16],
                                  xovT[:, i * GQ:(i + 1) * GQ],
                                  i == 0, i == NCH - 1),
                         reads=["satb", "xovT0", "xovT1"], writes=["psS0"])
                P.op("dve", lambda e: e.tensor_copy(
                    out=h96_sb[:], in_=psS[0][0:96, 0:256]),
                    reads=["psS0"], writes=["h96_sb"])
                P.op("dve", lambda e: e.tensor_copy(
                    out=hs_sb[:], in_=psS[0][0:16, 256:512]),
                    reads=["psS0"], writes=["hs_sb"])

                yield
                for qb in range(2):
                    for i in range(NCH):
                        P.op("pe", MM(
                            psS[1][:, qb * 6:(qb + 1) * 6],
                            xovT[:, i * GQ + qb * 128: i * GQ + qb * 128 + 128],
                            rwtb[:, i * 6:(i + 1) * 6], i == 0, False),
                            reads=["rwtb", f"xovT{qb}"], writes=["psS1"])
                    P.op("pe", MM(psS[1][:, qb * 6:(qb + 1) * 6], ones1p[:],
                                  rbrow[:], False, True),
                         reads=["ones1p", "rbrow"], writes=["psS1"])
                P.op("dve", lambda e: e.tensor_copy(
                    out=lg_sb[:], in_=psS[1][:, 0:12]),
                    reads=["psS1"], writes=["lg_sb"])

                yield
                P.op("dve", lambda e: e.reduce_max(
                    out=sm[:, 16:18],
                    in_=lg_sb[:, 0:12].rearrange("p (q s) -> p q s", q=2),
                    axis=AX.X),
                    reads=["lg_sb"], writes=["sm_mx"])
                P.op("dve", lambda e: e.tensor_scalar_mul(
                    out=sm[:, 18:20], in0=sm[:, 16:18], scalar1=-1.0),
                    reads=["sm_mx"], writes=["sm_nmx"])
                for qb in range(2):
                    P.op("act", lambda e, qb=qb: e.activation(
                        out=esb[:, qb * 6:(qb + 1) * 6],
                        in_=lg_sb[:, qb * 6:(qb + 1) * 6], func=AF.Exp,
                        bias=sm[:, 18 + qb:19 + qb]),
                        reads=["lg_sb", "sm_nmx"], writes=[f"esb{qb}"])
                P.op("dve", lambda e: e.reduce_sum(
                    out=sm[:, 20:22],
                    in_=esb[:, 0:12].rearrange("p (q s) -> p q s", q=2),
                    axis=AX.X),
                    reads=["esb0", "esb1"], writes=["sm_S"])
                P.op("dve", lambda e: e.tensor_scalar(
                    out=msk12[:], in0=esb[:, 0:12], scalar1=1.0,
                    scalar2=-1e30, op0=ALU.is_ge, op1=ALU.mult),
                    reads=["esb0", "esb1"], writes=["mh0", "mh1"])
                P.op("dve", lambda e: e.tensor_add(
                    out=msk12[:], in0=msk12[:], in1=esb[:, 0:12]),
                    reads=["mh0", "mh1", "esb0", "esb1"],
                    writes=["mh0", "mh1"])
                P.op("dve", lambda e: e.reduce_max(
                    out=sm[:, 22:24],
                    in_=msk12[:].rearrange("p (q s) -> p q s", q=2),
                    axis=AX.X),
                    reads=["mh0", "mh1"], writes=["sm_m2"])
                P.op("dve", lambda e: e.tensor_scalar(
                    out=sm[:, 24:26], in0=sm[:, 20:22], scalar1=1e-6,
                    scalar2=1.0, op0=ALU.mult, op1=ALU.add),
                    reads=["sm_S"], writes=["sm_den"])
                P.op("dve", lambda e: e.tensor_add(
                    out=sm[:, 24:26], in0=sm[:, 24:26], in1=sm[:, 22:24]),
                    reads=["sm_den", "sm_m2"], writes=["sm_den"])
                P.op("dve", lambda e: e.reciprocal(
                    out=sm[:, 26:28], in_=sm[:, 24:26]),
                    reads=["sm_den"], writes=["sm_rd"])
                for qb in range(2):
                    P.op("dve", lambda e, qb=qb: e.tensor_scalar(
                        out=msk12[:, qb * 6:(qb + 1) * 6],
                        in0=esb[:, qb * 6:(qb + 1) * 6],
                        scalar1=sm[:, 22 + qb:23 + qb],
                        scalar2=sm[:, 26 + qb:27 + qb],
                        op0=ALU.is_ge, op1=ALU.mult),
                        reads=[f"esb{qb}", "sm_m2", "sm_rd"],
                        writes=[f"mh{qb}"])
                    P.op("dve", lambda e, qb=qb: e.tensor_mul(
                        out=wtm[:, qb * 6:(qb + 1) * 6],
                        in0=msk12[:, qb * 6:(qb + 1) * 6],
                        in1=esb[:, qb * 6:(qb + 1) * 6]),
                        reads=[f"mh{qb}", f"esb{qb}"], writes=["wtm"])

                yield
                yield
                for qb in range(2):
                    P.op("pe", lambda t, qb=qb: t.transpose(
                        out=psT[0:6, qb * 128: qb * 128 + 128],
                        in_=wtm[:, qb * 6:(qb + 1) * 6], identity=identb[:]),
                        reads=["wtm", "identb"], writes=["psT"])
                for qb in range(2):
                    P.op("dve", lambda e, qb=qb: e.tensor_copy(
                        out=wfm[0:6, qb * 128: qb * 128 + 128],
                        in_=psT[0:6, qb * 128: qb * 128 + 128]),
                        reads=["psT"], writes=["wfm"])

                yield
                P.op("pe", MM(psS[0][0:96, 0:256], repb[:], wfm[:], True, True),
                     reads=["repb", "wfm"], writes=["psS0"])
                P.op("dve", lambda e: e.tensor_mul(
                    out=wh_sb[:], in0=psS[0][0:96, 0:256], in1=h96_sb[:]),
                    reads=["psS0", "h96_sb"], writes=["wh_sb"])

                yield
                for qb in range(2):
                    for h in range(2):
                        bidx = 2 * qb + h
                        sbank = "psS0" if bidx % 2 == 0 else "psS1"
                        bank = psS[bidx % 2]
                        for i in range(NCH):
                            P.op("pe", MM(
                                bank[:, 0:512],
                                xovT[:, i * GQ + qb * 128:
                                     i * GQ + qb * 128 + 128],
                                bwt[:, i * D + h * 512: i * D + (h + 1) * 512],
                                i == 0, False),
                                reads=[f"xovT{qb}", "bwt"], writes=[sbank])
                        P.op("pe", MM(bank[:, 0:512], ones1p[:],
                                      bbrow[0:1, h * 512:(h + 1) * 512],
                                      False, False),
                             reads=["ones1p", "bbrow"], writes=[sbank])
                        P.op("pe", MM(bank[:, 0:512],
                                      hs_sb[0:16, qb * 128: qb * 128 + 128],
                                      sbtb[0:16, h * 512:(h + 1) * 512],
                                      False, False),
                             reads=["hs_sb", "sbtb"], writes=[sbank])
                        P.op("pe", MM(bank[:, 0:512],
                                      wh_sb[0:96, qb * 128: qb * 128 + 128],
                                      rbtb[0:96, h * 512:(h + 1) * 512],
                                      False, True),
                             reads=["wh_sb", "rbtb"], writes=[sbank])
                        P.op("act", lambda e, qb=qb, h=h, bank=bank: e.activation(
                            out=astg[:, qb * D + h * 512: qb * D + (h + 1) * 512],
                            in_=bank[:, 0:512], func=AF.Copy),
                            reads=[sbank], writes=[f"astg{qb}"])
                        yield
                if dump:
                    P.op("pool", lambda e, g=g: e.dma_start(
                        out=dbg["dbg_f"][g], in_=astg[:]),
                        reads=["astg0", "astg1"], inc="dD_f", inc_by=16,
                        dma=True)

                yield
                for qb in range(2):
                    ln_tm(astg[:, qb * D:(qb + 1) * D], [f"astg{qb}"],
                          tscr[:], ["tscr"])
                    P.op("dve", lambda e, qb=qb: e.tensor_add(
                        out=x1buf[:, qb * D:(qb + 1) * D], in0=tscr[:],
                        in1=x1buf[:, qb * D:(qb + 1) * D]),
                        reads=["tscr", f"x1buf{qb}"], writes=[f"x1buf{qb}"])

                yield
                P.op("pool", lambda e, g=g: e.dma_start(
                    out=out[g].transpose([1, 0, 2]),
                    in_=x1buf[:].rearrange("p (s d) -> p s d", d=D)),
                    reads=["x1buf0", "x1buf1"], writes=[f"out{g}"],
                    inc="dOUT", inc_by=16, dma=True)

                yield

        pend = []

        def adv(n=1):
            for _ in range(n):
                while pend:
                    try:
                        next(pend[0][0])
                        break
                    except StopIteration:
                        pend.pop(0)

        def drain_epi():
            while pend:
                adv()

        for g in range(0 if p0_only else ng):
            gb = g % 2
            M = 4 * (g + 1)
            kbs = list(range(g + 1))
            if g % 2 == 1:
                kbs = kbs[::-1]
            slot_of = {}
            for kb in kbs[:RING - 1]:
                slot_of[kb] = tile_slot(kb, kbs)

            def plan_st(ms):
                kbi, sbi = ms // 4, ms % 4
                kb = kbs[kbi]
                if sbi == 0 and kbi + RING - 2 <= g:
                    k2 = kbs[kbi + RING - 2]
                    slot_of[k2] = tile_slot(k2, kbs[max(0, kbi - 1):])
                slot3 = slot_of[kb]
                m = mglob[0] + ms
                sbank = "psS0" if (m % 2 == 0) else "psS1"
                psSm = psS[m % 2]
                for i in range(NCH):
                    P.op("pe", MM(psSm[:, 0:256], kv_k(slot3, i, sbi),
                                  q_ap(gb, i), i == 0, i == NCH - 1),
                         reads=[f"kt{slot3}", f"qt{gb}"], writes=[sbank])
                P.op("act", lambda e, psSm=psSm, m=m: e.activation(
                    out=pt_ap(m), in_=psSm[:, 0:256], func=AF.Exp,
                    scale=SCALE, bias=cst[:, 1:2]),
                    reads=[sbank, "cst"], writes=[f"pt{m % PTRING}"])
                if kb == g:
                    P.op("dve", lambda e, m=m, sbi=sbi: e.tensor_mul(
                        out=pt_ap(m), in0=pt_ap(m),
                        in1=maskt[:, sbi * GQ:(sbi + 1) * GQ]),
                        reads=[f"pt{m % PTRING}", "maskt"],
                        writes=[f"pt{m % PTRING}"])

            def plan_av(ms):
                kbi, sbi = ms // 4, ms % 4
                kb = kbs[kbi]
                slot3 = slot_of[kb]
                m = mglob[0] + ms
                for qb in range(2):
                    for h in range(2):
                        P.op("pe", MM(
                            psBig[:, qb * D + h * 512:
                                  qb * D + (h + 1) * 512],
                            pt_ap(m, qb), kv_v(slot3, sbi, h),
                            ms == 0, ms == M - 1),
                            reads=[f"pt{m % PTRING}", f"vt{slot3}"],
                            writes=[BIGK[qb * 2 + h]])
                for qb in range(2):
                    col = ms * 2 + qb
                    P.op("pe", MM(psX[:, col:col + 1], pt_ap(m, qb),
                                  onescol[:], True, True),
                         reads=[f"pt{m % PTRING}", "onescol"],
                         writes=["psX"])

            LAG = 2
            for ms in range(M + LAG):
                if ms < M:
                    plan_st(ms)
                if ms >= LAG:
                    plan_av(ms - LAG)
                if ms % 2 == 0:
                    adv(1)
            mglob[0] += M
            while pend and pend[0][1] <= g - 2:
                try:
                    next(pend[0][0])
                except StopIteration:
                    pend.pop(0)
            ep = gb * 2
            P.op("dve", lambda e, M=M, ep=ep: e.reduce_sum(
                out=rsum[:, ep:ep + 2],
                in_=psX[:, 0:2 * M].rearrange("p (m q) -> p q m", q=2),
                axis=AX.X),
                reads=["psX"], writes=[f"rsum{gb}"])
            P.op("dve", lambda e, ep=ep: e.reciprocal(
                out=rsum[:, ep:ep + 2], in_=rsum[:, ep:ep + 2]),
                reads=[f"rsum{gb}"], writes=[f"rsum{gb}"])
            for qb in range(2):
                P.op("dve", lambda e, qb=qb, gb=gb, ep=ep:
                     e.scalar_tensor_tensor(
                    out=ovbf[:, gb * 2 * D + qb * D:
                             gb * 2 * D + (qb + 1) * D],
                    in0=psBig[:, qb * D:(qb + 1) * D],
                    scalar=rsum[:, ep + qb:ep + qb + 1],
                    in1=vown[:, gb * 2 * D + qb * D:
                             gb * 2 * D + (qb + 1) * D],
                    op0=ALU.mult, op1=ALU.add),
                    reads=[BIGK[qb * 2], BIGK[qb * 2 + 1], f"rsum{gb}",
                           f"vo{gb}"],
                    writes=[f"ovbf{gb}_{qb}"])
            if dump:
                P.op("pool", lambda e, g=g, gb=gb: e.dma_start(
                    out=dbg["dbg_ov"][g],
                    in_=ovbf[:, gb * 2 * D:(gb + 1) * 2 * D]),
                    reads=[f"ovbf{gb}_0", f"ovbf{gb}_1"], inc="dD_ov",
                    inc_by=16, dma=True)
            load_qv(g + 2)
            pend.append([epi_gen(g, gb, M), g])
        drain_epi()
        P.op("pool", lambda e: e.nop(),
             waits=[(s, P.cnt.get(s, 0)) for s in
                    ("dOUT", "dD_ov", "dD_xov", "dD_a", "dD_x1", "dD_f")])

        with nc.Block() as block:
            P.emit(block, sems)

    return nc


def build(ng=NG, n_slab_a=SEQ // KT, dump=False, p0_only=False):
    p1 = Plan(None)
    _plan(p1, ng, n_slab_a, dump, p0_only)
    p2 = Plan(p1.found)
    nc = _plan(p2, ng, n_slab_a, dump, p0_only)
    assert not (p2.found - p1.found), "scan missed deps"
    return nc



def _layouts(inputs, core):
    b, p = core // 2, core % 2
    x = np.asarray(inputs["L0_attn_x"], dtype=np.float32)
    xb = np.ascontiguousarray(x[b])
    rows = np.concatenate([
        np.arange(GQ) + GQ * (2 * j + p) for j in range(NG)])
    xq = np.ascontiguousarray(xb[rows])
    xkv = np.ascontiguousarray(
        xb.reshape(NG, 2, GQ, D)[:, [p, 1 - p]].reshape(SEQ, D))
    i256 = np.arange(GQ)
    mask = np.concatenate([
        (i256[:, None] <= i256[None, :]).astype(np.float32),
        np.full((GQ, GQ), float(p), dtype=np.float32)], axis=0)
    f32 = lambda a: np.ascontiguousarray(np.asarray(a, dtype=np.float32))
    def chunked(a):
        t = a.shape[0]
        return np.ascontiguousarray(
            a.T.reshape(NCH, 128, t // KT, KT).transpose(2, 1, 0, 3)
            .reshape(t // KT, 128, NCH * KT))
    return {
        "xkvC": chunked(xkv),
        "xqC": chunked(xq),
        "xq": xq,
        "wq": f32(inputs["L0_attn_Wq"]),
        "wk": f32(inputs["L0_attn_Wk"]),
        "wv": f32(inputs["L0_attn_Wv"]),
        "wfc": f32(inputs["L0_attn_Wfc"]),
        "bfc": f32(inputs["L0_attn_bfc"]).reshape(1, D),
        "gT": np.ascontiguousarray(
            f32(inputs["L0_attn_gamma"]).reshape(NCH, 128).T),
        "bT": np.ascontiguousarray(
            f32(inputs["L0_attn_beta"]).reshape(NCH, 128).T),
        "basewt": np.ascontiguousarray(f32(inputs["L0_ffn_base_W"]).T),
        "baseb": f32(inputs["L0_ffn_base_b"]).reshape(1, D),
        "sat": np.ascontiguousarray(f32(inputs["L0_ffn_shared_A"]).T),
        "sbt": np.ascontiguousarray(f32(inputs["L0_ffn_shared_B"]).T),
        "rat": np.ascontiguousarray(
            np.transpose(f32(inputs["L0_ffn_routed_A"]), (2, 0, 1)
                         ).reshape(D, 96)),
        "rbt": np.ascontiguousarray(
            np.transpose(f32(inputs["L0_ffn_routed_B"]), (0, 2, 1)
                         ).reshape(96, D)),
        "rwt": np.ascontiguousarray(f32(inputs["L0_ffn_router_W"]).T),
        "rb": f32(inputs["L0_ffn_router_b"]).reshape(1, 6),
        "rep": np.repeat(np.eye(6, dtype=np.float32), 16, axis=1),
        "ident": np.eye(128, dtype=np.float32),
        "maskT": mask,
    }


def kernel(**inputs):
    nc = build()
    in_maps = [_layouts(inputs, c) for c in range(NCORE)]
    res = run_bass_kernel_spmd(nc, in_maps, core_ids=list(range(NCORE)))
    outp = np.empty((NBATCH, SEQ, D), dtype=np.float32)
    for c in range(NCORE):
        b, p = c // 2, c % 2
        o = np.asarray(res.results[c]["out"]).reshape(NG, GQ, D)
        for j in range(NG):
            r0 = GQ * (2 * j + p)
            outp[b, r0:r0 + GQ] = o[j]
    return outp


# version 21
# speedup vs baseline: 1.0311x; 1.0311x over previous
import numpy as np
import concourse.bass as bass
import concourse.mybir as mybir
from concourse.bass_utils import run_bass_kernel_spmd
from contextlib import ExitStack

F32 = mybir.dt.float32
BF16 = mybir.dt.bfloat16
AF = mybir.ActivationFunctionType
ALU = mybir.AluOpType
AX = mybir.AxisListType

D = 1024
NCH = 8
SEQ = 8192
NBATCH = 4
NCORE = 8
GQ = 256
NG = 16
KT = 512
EPS = 1e-5
SCALE = 1.0 / 32.0
EXPB = -64.0
RING = 5
PTRING = 8
RACE_DETECT = False
TRUNC = None
FENCE_MODE = "full"

ENGINES = ("pe", "act", "dve", "sync", "pool")
ENG_SEM = {"pe": "pe", "act": "act", "dve": "dve", "sync": "msync",
           "pool": "mpool"}
DRAIN_ENGINES = {"act", "dve"}


class Plan:

    def __init__(self, needs=None):
        self.needs = needs
        self.found = set()
        self.ops = {e: [] for e in ENGINES}
        self.cnt = {}
        self.lastwait = {}
        self.seq = {e: 0 for e in ENGINES}
        self.handle = {}
        self.res = {}

    def op(self, eng, fn, reads=(), writes=(), inc=None, inc_by=1,
           waits=(), dma=False, drain=False):
        seq = self.seq[eng]
        self.seq[eng] += 1
        deps = list(waits)
        need_drain = drain

        def want(key):
            h = self.handle.get(key)
            if h is None:
                self.found.add(key)
            else:
                deps.append(h)

        for key in list(reads) + list(writes):
            st = self.res.get(key)
            if st is None:
                continue
            w = st["w"]
            if w is not None:
                if w[0] == eng:
                    if eng in DRAIN_ENGINES:
                        need_drain = True
                else:
                    want(w)
            if key in writes:
                for reng, rseq in st["r"].items():
                    if reng == eng:
                        if eng in DRAIN_ENGINES:
                            need_drain = True
                    else:
                        want((reng, rseq))
        if (self.needs is not None and inc is None
                and (eng, seq) in self.needs):
            inc = ENG_SEM[eng]
            if eng in DRAIN_ENGINES:
                need_drain = True
        val = None
        if inc is not None:
            self.cnt[inc] = self.cnt.get(inc, 0) + inc_by
            val = self.cnt[inc]
            self.handle[(eng, seq)] = (inc, val)
        else:
            self.handle[(eng, seq)] = None
        for key in reads:
            st = self.res.setdefault(key, {"w": None, "r": {}})
            st["r"][eng] = seq
        for key in writes:
            st = self.res.setdefault(key, {"w": None, "r": {}})
            st["w"] = (eng, seq)
            st["r"] = {}
        real = []
        for s, v in deps:
            if v is None or v <= 0:
                continue
            k = (eng, s)
            if self.lastwait.get(k, 0) >= v:
                continue
            self.lastwait[k] = v
            real.append((s, v))
        self.ops[eng].append((real, need_drain, fn, inc, inc_by))
        return val

    def fence(self, qwaits=None):
        qw = list(qwaits or ())
        if FENCE_MODE == "none":
            self.res = {}
            return
        for e in ("pe", "act", "dve"):
            if e == "dve":
                fn = lambda x: x.engine_nop()
                w = []
            elif e == "act":
                fn = lambda x: x.activation(out=self.junk[0:1, 0:1],
                                            in_=self.junk[0:1, 1:2],
                                            func=AF.Copy)
                w = ["junkt"]
            else:
                fn = lambda x: x.matmul(out=self.pejunk[0:1, 0:1],
                                        lhsT=self.peones[0:1, 0:1],
                                        rhs=self.peones[0:1, 0:1],
                                        start=True, stop=True)
                w = ["psX"]
            self.op(e, fn, writes=[f"_f1{e}"] + w,
                    reads=(["onescol"] if e == "pe" else []),
                    drain=(e in DRAIN_ENGINES))
        if FENCE_MODE == "f1":
            self.res = {}
            return
        for e in ENGINES:
            self.op(e, lambda x: x.nop(),
                    reads=["_f1pe", "_f1act", "_f1dve"], waits=qw)
        self.res = {}

    def emit(self, block, sems):
        def mk(name):
            def run(eng):
                for waits, drain, fn, inc, inc_by in self.ops[name]:
                    for s, v in waits:
                        eng.wait_ge(sems[s], v)
                    if drain:
                        eng.drain()
                    inst = fn(eng)
                    if inc is not None:
                        inst.then_inc(sems[inc], inc_by)
            return run
        block.tensor(mk("pe"))
        block.scalar(mk("act"))
        block.vector(mk("dve"))
        block.sync(mk("sync"))
        block.gpsimd(mk("pool"))


SEM_NAMES = (["pe", "act", "dve", "msync", "mpool", "dW", "dOUT"]
             + [f"q_kt{i}" for i in range(RING)]
             + [f"q_vt{i}" for i in range(RING)]
             + ["q_qt0", "q_qt1", "q_vo0", "q_vo1", "q_x0g0", "q_x0g1",
                "q_xs0", "q_xs1", "q_xsbf0", "q_xsbf1",
                "q_kstg", "q_vstg",
                "dD_ov", "dD_xov", "dD_a", "dD_x1", "dD_f"])


def _plan(P, ng, n_slab_a, dump, p0_only=False):
    nc = bass.Bass("TRN2", target_bir_lowering=False,
               detect_race_conditions=RACE_DETECT)
    scrkind = "ExternalOutput" if dump else "Internal"
    n_slab_b = (ng * GQ) // KT

    dr = {}
    def din(name, shape, dt=F32):
        dr[name] = nc.dram_tensor(name, shape, dt, kind="ExternalInput").ap()
    din("xkvC", [SEQ // KT, 128, NCH * KT])
    din("xqC", [NG * GQ // KT, 128, NCH * KT])
    din("xq", [NG * GQ, D])
    din("wq", [D, D]); din("wk", [D, D]); din("wv", [D, D]); din("wfc", [D, D])
    din("bfc", [1, D])
    din("gT", [128, NCH])
    din("bT", [128, NCH])
    din("basewt", [D, D])
    din("baseb", [1, D])
    din("sat", [D, 16])
    din("sbt", [16, D])
    din("rat", [D, 96])
    din("rbt", [96, D])
    din("rwt", [D, 6])
    din("rb", [1, 6])
    din("rep", [6, 96])
    din("ident", [128, 128])
    din("maskT", [KT, GQ])

    out = nc.dram_tensor("out", [NG, 2, 128, D], F32,
                         kind="ExternalOutput").ap()
    dbg = {}
    if dump:
        for nm in ("dbg_ov", "dbg_x1", "dbg_f", "dbg_xov", "dbg_a"):
            dbg[nm] = nc.dram_tensor(nm, [NG, 128, 2 * D], F32,
                                     kind="ExternalOutput").ap()

    kscr = nc.dram_tensor("kscr", [SEQ // KT, 128, NCH * KT], BF16,
                          kind=scrkind).ap()
    vscr = nc.dram_tensor("vscr", [SEQ // KT, 128, 4 * D], BF16,
                          kind=scrkind).ap()
    qscr = nc.dram_tensor("qscr", [NG, 128, NCH * GQ], BF16,
                          kind=scrkind).ap()
    with ExitStack() as st:
        def sb(name, shape, dt):
            return st.enter_context(nc.sbuf_tensor("s_" + name, shape, dt))
        def psal(name, shape, dt=F32):
            return st.enter_context(nc.psum_tensor(name, shape, dt))

        psS = [psal("psS0", [128, 512]), psal("psS1", [128, 512])]
        psBig = psal("psBig", [128, 2048])
        psX = psal("psX", [128, 512])
        psT = psal("psT", [128, 1024], BF16)
        BIGK = ["psB0", "psB1", "psB2", "psB3"]

        wfc = sb("wfc", [128, NCH * D], BF16)
        bwt = sb("bwt", [128, NCH * D], BF16)
        satb = sb("satb", [128, NCH * 16], BF16)
        sbtb = sb("sbtb", [16, D], BF16)
        ratb = sb("ratb", [128, NCH * 96], BF16)
        rbtb = sb("rbtb", [96, D], BF16)
        rwtb = sb("rwtb", [128, NCH * 6], BF16)
        rbrow = sb("rbrow", [1, 6], BF16)
        repb = sb("repb", [6, 96], BF16)
        identb = sb("identb", [128, 128], BF16)
        maskt = sb("maskt", [128, 4 * GQ], BF16)
        gamT = sb("gamT", [128, NCH], F32)
        betT = sb("betT", [128, NCH], F32)
        bt8 = sb("bt8", [128, NCH], BF16)
        bfcrow = sb("bfcrow", [1, D], F32)
        bbrow = sb("bbrow", [1, D], BF16)
        ccbf = sb("ccbf", [1, D], BF16)
        onesf = sb("onesf", [1, 128], F32)
        ones1p = sb("ones1p", [1, 128], BF16)
        onescol = sb("onescol", [128, 1], BF16)
        cst = sb("cst", [128, 2], F32)
        junk = sb("junk", [1, 2], F32)
        ccb_bc = sb("ccb_bc", [128, D], BF16)
        bb_bc = sb("bb_bc", [128, D], BF16)
        P.junk = junk
        P.peones = onescol
        P.pejunk = psX

        sems = {n: st.enter_context(nc.semaphore(n)) for n in SEM_NAMES}

        def MM(o, l, r, start, stop):
            return lambda t: t.matmul(out=o, lhsT=l, rhs=r,
                                      start=start, stop=stop)

        P.op("dve", lambda e: e.memset(onesf[:], 1.0), writes=["onesf"])
        P.op("dve", lambda e: e.memset(ones1p[:], 1.0), writes=["ones1p"])
        P.op("dve", lambda e: e.memset(onescol[:], 1.0), writes=["onescol"])
        P.op("dve", lambda e: e.memset(cst[:, 0:1], EPS), writes=["cst"])
        P.op("dve", lambda e: e.memset(junk[:], 0.0), writes=["junkt"])
        P.op("dve", lambda e: e.memset(cst[:, 1:2], EXPB), writes=["cst"])

        def WL(dst, src, wkey):
            P.op("pool", lambda e, d=dst, s=src: e.dma_start(out=d, in_=s),
                 writes=[wkey], inc="dW", inc_by=16, dma=True)

        def rr(a):
            return a.rearrange("(c p) o -> p c o", p=128)

        def chv(t, x):
            return t[:].rearrange("p (c o) -> p c o", o=x)

        epiw = [
            lambda: WL(chv(wfc, D), rr(dr["wfc"]), "wfc"),
            lambda: WL(chv(bwt, D), rr(dr["basewt"]), "bwt"),
            lambda: WL(chv(satb, 16), rr(dr["sat"]), "satb"),
            lambda: WL(sbtb[:], dr["sbt"], "sbtb"),
            lambda: WL(chv(ratb, 96), rr(dr["rat"]), "ratb"),
            lambda: WL(rbtb[:], dr["rbt"], "rbtb"),
            lambda: WL(chv(rwtb, 6), rr(dr["rwt"]), "rwtb"),
            lambda: WL(rbrow[:], dr["rb"], "rbrow"),
            lambda: WL(repb[:], dr["rep"], "repb"),
            lambda: WL(identb[:], dr["ident"], "identb"),
            lambda: WL(maskt[:].rearrange("p (s q) -> p s q", q=GQ),
                       dr["maskT"].rearrange("(s p) q -> p s q", p=128),
                       "maskt"),
            lambda: WL(gamT[:], dr["gT"], "gamT"),
            lambda: WL(betT[:], dr["bT"], "betT"),
            lambda: WL(bfcrow[:], dr["bfc"], "bfcrow"),
            lambda: WL(bbrow[:], dr["baseb"], "bbrow"),
        ]

        def plan_epi_fence():
            P.op("pool", lambda e: e.nop(),
                 waits=[("dW", P.cnt["dW"])],
                 writes=["wfc", "bwt", "satb", "sbtb", "ratb", "rbtb",
                         "rwtb", "rbrow", "repb", "identb", "maskt", "gamT",
                         "betT", "bfcrow", "bbrow"])

        def plan_cconst():
            P.op("dve", lambda e: e.tensor_copy(out=bt8[:], in_=betT[:]),
                 reads=["betT"], writes=["bt8"])
            for h in range(2):
                for i in range(NCH):
                    P.op("pe", MM(
                        psX[0:1, 0:512], bt8[:, i:i + 1],
                        wfc[:, i * D + h * 512: i * D + (h + 1) * 512],
                        i == 0, i == NCH - 1),
                        reads=["bt8", "wfc"], writes=["psX"])
                P.op("dve", lambda e, h=h: e.tensor_add(
                    out=ccbf[0:1, h * 512:(h + 1) * 512],
                    in0=psX[0:1, 0:512],
                    in1=bfcrow[0:1, h * 512:(h + 1) * 512]),
                    reads=["psX", "bfcrow"], writes=["ccbf"])
            for h in range(2):
                P.op("pe", MM(psX[:, 0:512], ones1p[:],
                              ccbf[0:1, h * 512:(h + 1) * 512], True, True),
                     reads=["ones1p", "ccbf"], writes=["psX"])
                P.op("dve", lambda e, h=h: e.tensor_copy(
                    out=ccb_bc[:, h * 512:(h + 1) * 512], in_=psX[:, 0:512]),
                    reads=["psX"], writes=["ccb_bc"])
                P.op("pe", MM(psX[:, 0:512], ones1p[:],
                              bbrow[0:1, h * 512:(h + 1) * 512], True, True),
                     reads=["ones1p", "bbrow"], writes=["psX"])
                P.op("dve", lambda e, h=h: e.tensor_copy(
                    out=bb_bc[:, h * 512:(h + 1) * 512], in_=psX[:, 0:512]),
                    reads=["psX"], writes=["bb_bc"])
            for i in range(NCH):
                P.op("dve", lambda e, i=i: e.tensor_scalar_mul(
                    out=wfc[:, i * D:(i + 1) * D],
                    in0=wfc[:, i * D:(i + 1) * D],
                    scalar1=gamT[:, i:i + 1]),
                    reads=["wfc", "gamT"], writes=["wfc"])

        with ExitStack() as st0:
            def sb0(name, shape, dt):
                return st0.enter_context(
                    nc.sbuf_tensor("s_" + name, shape, dt))
            wq = sb0("wq", [128, NCH * D], BF16)
            wk = sb0("wk", [128, NCH * D], BF16)
            wv = sb0("wv", [128, NCH * D], BF16)
            cwq = sb0("cwq", [1, D], BF16)
            cwk = sb0("cwk", [1, D], BF16)
            cwv = sb0("cwv", [1, D], BF16)
            xs32 = sb0("xs32", [128, 2 * NCH * 512], F32)
            xsbf = sb0("xsbf", [128, NCH * 512], BF16)
            xsq = sb0("xsq", [128, NCH * 512], BF16)
            xhat = sb0("xhat", [128, 2 * NCH * 512], BF16)
            kstg = sb0("kstg", [128, NCH * 512], BF16)
            vstg = sb0("vstg", [128, 4 * D], BF16)
            r_mu = sb0("r_mu", [1, 512], F32)
            r_e2 = sb0("r_e2", [1, 512], F32)
            r_mu2 = sb0("r_mu2", [1, 512], F32)
            r_var = sb0("r_var", [1, 512], F32)
            r_r = sb0("r_r", [1, 512], F32)
            r_mr = sb0("r_mr", [1, 512], F32)
            r_mrb = sb0("r_mrb", [1, 1024], BF16)
            hold_r = sb0("hold_r", [1, 512], F32)
            hold_m = sb0("hold_m", [1, 512], BF16)

            def plan_cw(wten, crow, wkey, ckey):
                for h in range(2):
                    for i in range(NCH):
                        P.op("pe", MM(
                            psX[0:1, 0:512], onescol[:],
                            wten[:, i * D + h * 512: i * D + (h + 1) * 512],
                            i == 0, i == NCH - 1),
                            reads=["onescol", wkey], writes=["psX"])
                    P.op("dve", lambda e, crow=crow, h=h: e.tensor_scalar_mul(
                        out=crow[0:1, h * 512:(h + 1) * 512],
                        in0=psX[0:1, 0:512], scalar1=-1.0),
                        reads=["psX"], writes=[ckey])

            def p0_dma_xs(sl):
                srcC, tok_off, kdst, vdst, is_q, kdkeys, vdkeys, idx = sl
                buf = idx % 2
                xo = buf * NCH * 512
                src_ = srcC[tok_off // KT]
                P.op("sync", lambda e: e.dma_start(
                    out=xs32[:, xo:xo + NCH * 512], in_=src_),
                    writes=[f"xs{buf}"], inc=f"q_xs{buf}", inc_by=16,
                    dma=True)

            def p0_dma_xsbf(sl):
                srcC, tok_off, kdst, vdst, is_q, kdkeys, vdkeys, idx = sl
                if is_q:
                    return
                buf = idx % 2
                src_ = srcC[tok_off // KT]
                P.op("pool", lambda e: e.dma_start(
                    out=xsbf[:], in_=src_),
                    writes=["xsbf"], inc=f"q_xsbf{buf}", inc_by=16,
                    dma=True)

            def p0_stats_mm(sl):
                srcT, tok_off, kdst, vdst, is_q, kdkeys, vdkeys, idx = sl
                if is_q:
                    return
                buf = idx % 2
                xo = buf * NCH * 512
                xq_ = xsq
                P.op("act", lambda e: e.activation(
                    out=xq_[:], in_=xs32[:, xo:xo + NCH * 512],
                    func=AF.Square), reads=[f"xs{buf}"],
                    writes=["xsq"])
                for i in range(NCH):
                    P.op("pe", MM(psX[0:1, 0:512], onescol[:],
                                  xsbf[:, i * 512:(i + 1) * 512],
                                  i == 0, i == NCH - 1),
                         reads=["onescol", "xsbf"], writes=["psX"])
                for i in range(NCH):
                    P.op("pe", MM(psS[0][0:1, 0:512], onescol[:],
                                  xq_[:, i * 512:(i + 1) * 512],
                                  i == 0, i == NCH - 1),
                         reads=["onescol", "xsq"], writes=["psS0"])

            def p0_rows(sl):
                srcT, tok_off, kdst, vdst, is_q, kdkeys, vdkeys, idx = sl
                buf = idx % 2
                if is_q:
                    rmrb_ = r_mrb[0:1, buf * 512:(buf + 1) * 512]
                    P.op("dve", lambda e: e.tensor_copy(
                        out=r_r[0:1, 0:512], in_=hold_r[0:1, 0:512]),
                        reads=["hold_r0", "hold_r1"], writes=["r_r"])
                    P.op("dve", lambda e, rmrb_=rmrb_: e.tensor_copy(
                        out=rmrb_, in_=hold_m[0:1, 0:512]),
                        reads=["hold_m0", "hold_m1"], writes=[f"r_mrb{buf}"])
                    return
                rmu = r_mu[0:1, 0:512]
                rvv = r_var[0:1, 0:512]
                rrr = r_r[0:1, 0:512]
                rmrb = r_mrb[0:1, buf * 512:(buf + 1) * 512]
                P.op("dve", lambda e: e.tensor_scalar_mul(
                    out=rmu, in0=psX[0:1, 0:512], scalar1=1.0 / D),
                    reads=["psX"], writes=["r_mu"])
                P.op("dve", lambda e: e.tensor_scalar_mul(
                    out=r_e2[0:1, 0:512], in0=psS[0][0:1, 0:512],
                    scalar1=1.0 / D),
                    reads=["psS0"], writes=["r_e2"])
                P.op("act", lambda e: e.activation(
                    out=r_mu2[0:1, 0:512], in_=rmu, func=AF.Square),
                    reads=["r_mu"], writes=["r_mu2"])
                P.op("dve", lambda e: e.tensor_sub(
                    out=rvv, in0=r_e2[0:1, 0:512], in1=r_mu2[0:1, 0:512]),
                    reads=["r_e2", "r_mu2"], writes=["r_var"])
                P.op("act", lambda e: e.activation(
                    out=rvv, in_=rvv, func=AF.Ln, bias=cst[0:1, 0:1]),
                    reads=["r_var", "cst"], writes=["r_var"])
                P.op("act", lambda e: e.activation(
                    out=rrr, in_=rvv, func=AF.Exp, scale=-0.5),
                    reads=["r_var"], writes=["r_r"])
                P.op("dve", lambda e: e.tensor_mul(
                    out=r_mr[0:1, 0:512], in0=rmu, in1=rrr),
                    reads=["r_mu", "r_r"], writes=["r_mr"])
                P.op("dve", lambda e: e.tensor_copy(out=rmrb,
                                                    in_=r_mr[0:1, 0:512]),
                     reads=["r_mr"], writes=[f"r_mrb{buf}"])
                hp = (tok_off // KT) % 2
                P.op("dve", lambda e, hp=hp: e.tensor_copy(
                    out=hold_r[0:1, hp * 256:(hp + 1) * 256],
                    in_=rrr[0:1, 0:256]),
                    reads=["r_r"], writes=[f"hold_r{hp}"])
                P.op("dve", lambda e, hp=hp: e.tensor_copy(
                    out=hold_m[0:1, hp * 256:(hp + 1) * 256],
                    in_=r_mr[0:1, 0:256]),
                    reads=["r_mr"], writes=[f"hold_m{hp}"])

            def p0_xhat(sl):
                srcT, tok_off, kdst, vdst, is_q, kdkeys, vdkeys, idx = sl
                buf = idx % 2
                xo = buf * NCH * 512
                rrr = r_r[0:1, 0:512]
                P.op("pe", MM(psS[1][:, 0:512], onesf[:], rrr, True, True),
                     reads=["onesf", "r_r"], writes=["psS1"])
                for i in range(NCH):
                    P.op("dve", lambda e, i=i: e.tensor_mul(
                        out=xhat[:, buf * NCH * 512 + i * 512:
                                 buf * NCH * 512 + (i + 1) * 512],
                        in0=xs32[:, xo + i * 512: xo + (i + 1) * 512],
                        in1=psS[1][:, 0:512]),
                        reads=[f"xs{buf}", "psS1"],
                        writes=[f"xhat{buf}_{i}"])

            def p0_proj(sl):
                srcT, tok_off, kdst, vdst, is_q, kdkeys, vdkeys, idx = sl
                buf = idx % 2
                xh = lambda i: xhat[:, buf * NCH * 512 + i * 512:
                                    buf * NCH * 512 + (i + 1) * 512]
                rmrb = r_mrb[0:1, buf * 512:(buf + 1) * 512]
                wten = wq if is_q else wk
                wkey = "wq" if is_q else "wk"
                crow = cwq if is_q else cwk
                ckey = "cwq" if is_q else "cwk"
                for o in range(NCH):
                    bank = BIGK[o % 4]
                    for i in range(NCH):
                        P.op("pe", MM(
                            psBig[:, (o % 4) * 512:(o % 4 + 1) * 512],
                            wten[:, i * D + o * 128: i * D + (o + 1) * 128],
                            xh(i), i == 0, False),
                            reads=[wkey, f"xhat{buf}_{i}"], writes=[bank])
                    P.op("pe", MM(psBig[:, (o % 4) * 512:(o % 4 + 1) * 512],
                                  crow[0:1, o * 128:(o + 1) * 128], rmrb,
                                  False, True),
                         reads=[ckey, f"r_mrb{buf}"], writes=[bank])
                    P.op("dve", lambda e, o=o: e.tensor_copy(
                        out=kstg[:, o * 512:(o + 1) * 512],
                        in_=psBig[:, (o % 4) * 512:(o % 4 + 1) * 512]),
                        reads=[bank], writes=["kstg"])
                if is_q:
                    t2 = tok_off // KT
                    for u in range(2):
                        P.op("sync", lambda e, u=u, t2=t2: e.dma_start(
                            out=qscr[2 * t2 + u].rearrange(
                                "p (c q) -> p c q", q=GQ),
                            in_=kstg[:].rearrange(
                                "p (c t) -> p c t", t=512)[:, :, u * GQ:
                                                           (u + 1) * GQ]),
                            reads=["kstg"], writes=[kdkeys[u]],
                            inc="q_kstg", inc_by=16, dma=True)
                else:
                    P.op("sync", lambda e, kdst=kdst: e.dma_start(
                        out=kdst, in_=kstg[:]),
                        reads=["kstg"], writes=list(kdkeys),
                        inc="q_kstg", inc_by=16, dma=True)
                if is_q:
                    return
                for s4 in range(4):
                    for h in range(2):
                        bank = BIGK[(s4 * 2 + h) % 4]
                        bs = ((s4 * 2 + h) % 4) * 512
                        for i in range(NCH):
                            P.op("pe", MM(
                                psBig[:, bs:bs + 512],
                                xh(i)[:, s4 * 128: s4 * 128 + 128],
                                wv[:, i * D + h * 512: i * D + (h + 1) * 512],
                                i == 0, False),
                                reads=[f"xhat{buf}_{i}", "wv"],
                                writes=[bank])
                        P.op("pe", MM(psBig[:, bs:bs + 512],
                                      rmrb[0:1, s4 * 128: s4 * 128 + 128],
                                      cwv[0:1, h * 512:(h + 1) * 512],
                                      False, True),
                             reads=[f"r_mrb{buf}", "cwv"], writes=[bank])
                        P.op("dve", lambda e, s4=s4, h=h, bs=bs:
                             e.tensor_copy(
                            out=vstg[:, s4 * D + h * 512:
                                     s4 * D + (h + 1) * 512],
                            in_=psBig[:, bs:bs + 512]),
                            reads=[bank], writes=["vstg"])
                P.op("sync", lambda e, vdst=vdst: e.dma_start(
                    out=vdst, in_=vstg[:]),
                    reads=["vstg"], writes=list(vdkeys),
                    inc="q_vstg", inc_by=16, dma=True)

            order = []
            for s in range(n_slab_b):
                order += [("kv", 2 * s), ("kv", 2 * s + 1), ("q", s)]
            for t in range(2 * n_slab_b, n_slab_a):
                order.append(("kv", t))
            slabs = []
            for idx, (kind, t) in enumerate(order):
                if kind == "kv":
                    slabs.append((dr["xkvC"], t * KT, kscr[t], vscr[t],
                                  False, [f"kscr{t}"], [f"vscr{t}"], idx))
                else:
                    slabs.append((dr["xqC"], t * KT, None, None, True,
                                  [f"qscr{2 * t}", f"qscr{2 * t + 1}"],
                                  [], idx))
            nsl = len(slabs)
            if nsl:
                p0_dma_xs(slabs[0])
                p0_dma_xsbf(slabs[0])
                if nsl > 1:
                    p0_dma_xs(slabs[1])
                WL(chv(wk, D), rr(dr["wk"]), "wk")
                WL(chv(wv, D), rr(dr["wv"]), "wv")
                WL(chv(wq, D), rr(dr["wq"]), "wq")
                P.op("pool", lambda e: e.nop(),
                     waits=[("dW", P.cnt["dW"])], writes=["wq", "wk", "wv"])
                p0_stats_mm(slabs[0])
                p0_rows(slabs[0])
                plan_cw(wk, cwk, "wk", "cwk")
                p0_xhat(slabs[0])
                plan_cw(wv, cwv, "wv", "cwv")
                for t in range(nsl):
                    if t + 1 < nsl:
                        p0_dma_xsbf(slabs[t + 1])
                        p0_stats_mm(slabs[t + 1])
                        p0_rows(slabs[t + 1])
                    if t + 2 < nsl:
                        p0_dma_xs(slabs[t + 2])
                    p0_proj(slabs[t])
                    if t == 0:
                        plan_cw(wq, cwq, "wq", "cwq")
                    if t >= 1 and epiw:
                        epiw.pop(0)()
                        if len(epiw) >= 10:
                            epiw.pop(0)()
                    if t + 1 < nsl:
                        p0_xhat(slabs[t + 1])
                while epiw:
                    epiw.pop(0)()
                plan_epi_fence()
                plan_cconst()

        P.fence(qwaits=[(s, P.cnt.get(s, 0)) for s in
                        ("q_kstg", "q_vstg", "q_xsbf0", "q_xsbf1", "dW",
                         "q_xs0", "q_xs1")])

        ktile = sb("ktile", [128, RING * NCH * KT], BF16)
        vtile = sb("vtile", [128, RING * 4 * D], BF16)
        qtile = sb("qtile", [128, 2 * NCH * GQ], BF16)
        vown = sb("vown", [128, 2 * 2 * D], BF16)
        x0g = sb("x0g", [128, 2 * 2 * D], F32)
        ptb = sb("ptb", [128, PTRING * GQ], BF16)
        ovbf = sb("ovbf", [128, 2 * 2 * D], BF16)
        xov = sb("xov", [128, 2 * D], BF16)
        xovT = sb("xovT", [128, NCH * GQ], BF16)
        sqbuf = sb("sqbuf", [128, D], BF16)
        astg = sb("astg", [128, 2 * D], F32)
        x1buf = sb("x1buf", [128, 2 * D], F32)
        hs_sb = sb("hs_sb", [16, GQ], BF16)
        h96_sb = sb("h96_sb", [96, GQ], BF16)
        lg_sb = sb("lg_sb", [128, 12], F32)
        esb = sb("esb", [128, 12], F32)
        msk12 = sb("msk12", [128, 12], F32)
        wtm = sb("wtm", [128, 12], BF16)
        wfm = sb("wfm", [6, GQ], BF16)
        wh_sb = sb("wh_sb", [96, GQ], BF16)
        sm = sb("sm", [128, 64], F32)
        rsum = sb("rsum", [128, 4], F32)

        def kv_k(slot, i, sbi):
            off = slot * NCH * KT + i * KT + sbi * 128
            return ktile[:, off:off + 128]

        def kv_v(slot, sbi, h):
            off = slot * 4 * D + sbi * D + h * 512
            return vtile[:, off:off + 512]

        def q_ap(bufi, i):
            off = bufi * NCH * GQ + i * GQ
            return qtile[:, off:off + GQ]

        def pt_ap(m, qb=None):
            off = (m % PTRING) * GQ + (0 if qb is None else qb * 128)
            return ptb[:, off:off + (GQ if qb is None else 128)]

        mglob = [0]
        tcache = {}
        tclock = [0]

        def tile_slot(kb, future):
            for s, k in tcache.items():
                if k == kb:
                    return s
            if len(tcache) < RING:
                s = len(tcache)
            else:
                def nxt(k):
                    try:
                        return future.index(k)
                    except ValueError:
                        return 10 ** 9
                s = max(tcache, key=lambda sl: nxt(tcache[sl]))
            tcache[s] = kb
            P.op("sync", lambda e, kb=kb, s=s: e.dma_start(
                out=ktile[:, s * NCH * KT:(s + 1) * NCH * KT],
                in_=kscr[kb]),
                reads=[f"kscr{kb}"], writes=[f"kt{s}"],
                inc=f"q_kt{s}", inc_by=16, dma=True)
            P.op("sync", lambda e, kb=kb, s=s: e.dma_start(
                out=vtile[:, s * 4 * D:(s + 1) * 4 * D],
                in_=vscr[kb]),
                reads=[f"vscr{kb}"], writes=[f"vt{s}"],
                inc=f"q_vt{s}", inc_by=16, dma=True)
            return s

        def load_qv(g):
            if g >= ng:
                return
            gb = g % 2
            P.op("pool", lambda e, g=g, gb=gb: e.dma_start(
                out=qtile[:, gb * NCH * GQ:(gb + 1) * NCH * GQ],
                in_=qscr[g]),
                reads=[f"qscr{g}"], writes=[f"qt{gb}"],
                inc=f"q_qt{gb}", inc_by=16, dma=True)
            P.op("pool", lambda e, g=g, gb=gb: e.dma_start(
                out=vown[:, gb * 2 * D:(gb + 1) * 2 * D],
                in_=vscr[g][:, 0:2 * D]),
                reads=[f"vscr{g}"], writes=[f"vo{gb}"],
                inc=f"q_vo{gb}", inc_by=16, dma=True)

        def load_x0(g):
            if g >= ng:
                return
            gb = g % 2
            P.op("pool", lambda e, g=g, gb=gb: e.dma_start(
                out=x0g[:, gb * 2 * D:(gb + 1) * 2 * D].rearrange(
                    "p (s d) -> p s d", d=D),
                in_=dr["xq"][g * GQ:(g + 1) * GQ, :].rearrange(
                    "(s p) d -> p s d", p=128)),
                writes=[f"x0g{gb}"], inc=f"q_x0g{gb}", inc_by=16,
                dma=True)

        def ln_tm(src_ap, src_keys, dst_ap, dst_keys, reg=0):
            o = reg * 32
            P.op("dve", lambda e, o=o: e.bn_stats(
                out=sm[:, o:o + 6], in_=src_ap[:, 0:512]),
                reads=list(src_keys), writes=[f"sm_b0_{reg}"])
            P.op("dve", lambda e, o=o: e.bn_stats(
                out=sm[:, o + 6:o + 12], in_=src_ap[:, 512:1024]),
                reads=list(src_keys), writes=[f"sm_b1_{reg}"])
            P.op("dve", lambda e, o=o: e.bn_aggr(
                out=sm[:, o + 12:o + 14], in_=sm[:, o:o + 12]),
                reads=[f"sm_b0_{reg}", f"sm_b1_{reg}"],
                writes=[f"sm_mv_{reg}"])
            P.op("act", lambda e, o=o: e.activation(
                out=sm[:, o + 14:o + 15], in_=sm[:, o + 13:o + 14],
                func=AF.Ln, bias=cst[:, 0:1]),
                reads=[f"sm_mv_{reg}", "cst"], writes=[f"sm_ln_{reg}"])
            P.op("act", lambda e, o=o: e.activation(
                out=sm[:, o + 15:o + 16], in_=sm[:, o + 14:o + 15],
                func=AF.Exp, scale=-0.5),
                reads=[f"sm_ln_{reg}"], writes=[f"sm_r_{reg}"])
            P.op("dve", lambda e, o=o: e.tensor_scalar(
                out=dst_ap, in0=src_ap, scalar1=sm[:, o + 12:o + 13],
                scalar2=sm[:, o + 15:o + 16], op0=ALU.subtract,
                op1=ALU.mult),
                reads=list(src_keys) + [f"sm_mv_{reg}", f"sm_r_{reg}"],
                writes=list(dst_keys))

        def transpose8(src, src_keys, qb):
            for i in range(NCH):
                P.op("pe", lambda t, i=i: t.transpose(
                    out=psT[:, i * 128:(i + 1) * 128],
                    in_=src[:, i * 128:(i + 1) * 128], identity=identb[:]),
                    reads=list(src_keys) + ["identb"], writes=["psT"])
            for i in range(NCH):
                P.op("dve", lambda e, i=i, qb=qb: e.tensor_copy(
                    out=xovT[:, i * GQ + qb * 128: i * GQ + qb * 128 + 128],
                    in_=psT[:, i * 128:(i + 1) * 128]),
                    reads=["psT"], writes=[f"xovT{qb}"])

        load_qv(0)
        load_qv(1)
        load_x0(0)
        load_x0(1)

        def epi_gen(g, gb, M):
                for qb in range(2):
                    ln_tm(ovbf[:, gb * 2 * D + qb * D:
                               gb * 2 * D + (qb + 1) * D],
                          [f"ovbf{gb}_{qb}"],
                          xov[:, qb * D:(qb + 1) * D], [f"xov{qb}"],
                          reg=qb)
                    yield
                    transpose8(xov[:, qb * D:(qb + 1) * D], [f"xov{qb}"], qb)
                if dump:
                    P.op("pool", lambda e, g=g: e.dma_start(
                        out=dbg["dbg_xov"][g], in_=xov[:]),
                        reads=["xov0", "xov1"], inc="dD_xov", inc_by=16,
                        dma=True)

                yield
                for qb in range(2):
                    for h in range(2):
                        bidx = 2 * qb + h
                        sbank = "psS0" if bidx % 2 == 0 else "psS1"
                        bank = psS[bidx % 2]
                        for i in range(NCH):
                            P.op("pe", MM(
                                bank[:, 0:512],
                                xovT[:, i * GQ + qb * 128:
                                     i * GQ + qb * 128 + 128],
                                wfc[:, i * D + h * 512: i * D + (h + 1) * 512],
                                i == 0, False),
                                reads=[f"xovT{qb}", "wfc"], writes=[sbank])
                        P.op("pe", MM(bank[:, 0:512], ones1p[:],
                                      ccbf[0:1, h * 512:(h + 1) * 512],
                                      False, True),
                             reads=["ones1p", "ccbf"], writes=[sbank])
                        P.op("act", lambda e, qb=qb, h=h, bank=bank: e.activation(
                            out=astg[:, qb * D + h * 512: qb * D + (h + 1) * 512],
                            in_=bank[:, 0:512], func=AF.Copy),
                            reads=[sbank], writes=[f"astg{qb}"])
                        yield
                if dump:
                    P.op("pool", lambda e, g=g: e.dma_start(
                        out=dbg["dbg_a"][g], in_=astg[:]),
                        reads=["astg0", "astg1"], inc="dD_a", inc_by=16,
                        dma=True)

                yield
                for qb in range(2):
                    ln_tm(astg[:, qb * D:(qb + 1) * D], [f"astg{qb}"],
                          xov[:, qb * D:(qb + 1) * D], [f"xov{qb}"],
                          reg=qb)
                for qb in range(2):
                    P.op("dve", lambda e, qb=qb, gb=gb: e.tensor_add(
                        out=x1buf[:, qb * D:(qb + 1) * D],
                        in0=xov[:, qb * D:(qb + 1) * D],
                        in1=x0g[:, gb * 2 * D + qb * D:
                                gb * 2 * D + (qb + 1) * D]),
                        reads=[f"xov{qb}", f"x0g{gb}"],
                        writes=[f"x1buf{qb}"])
                load_x0(g + 2)
                if dump:
                    P.op("pool", lambda e, g=g: e.dma_start(
                        out=dbg["dbg_x1"][g], in_=x1buf[:]),
                        reads=["x1buf0", "x1buf1"], inc="dD_x1", inc_by=16,
                        dma=True)

                yield
                for qb in range(2):
                    ln_tm(x1buf[:, qb * D:(qb + 1) * D], [f"x1buf{qb}"],
                          xov[:, qb * D:(qb + 1) * D], [f"xov{qb}"],
                          reg=qb)
                    yield
                    transpose8(xov[:, qb * D:(qb + 1) * D], [f"xov{qb}"], qb)

                yield
                for i in range(NCH):
                    P.op("pe", MM(psS[0][0:96, 0:256],
                                  ratb[:, i * 96:(i + 1) * 96],
                                  xovT[:, i * GQ:(i + 1) * GQ],
                                  i == 0, i == NCH - 1),
                         reads=["ratb", "xovT0", "xovT1"], writes=["psS0"])
                for i in range(NCH):
                    P.op("pe", MM(psS[0][0:16, 256:512],
                                  satb[:, i * 16:(i + 1) * 16],
                                  xovT[:, i * GQ:(i + 1) * GQ],
                                  i == 0, i == NCH - 1),
                         reads=["satb", "xovT0", "xovT1"], writes=["psS0"])
                P.op("dve", lambda e: e.tensor_copy(
                    out=h96_sb[:], in_=psS[0][0:96, 0:256]),
                    reads=["psS0"], writes=["h96_sb"])
                P.op("dve", lambda e: e.tensor_copy(
                    out=hs_sb[:], in_=psS[0][0:16, 256:512]),
                    reads=["psS0"], writes=["hs_sb"])

                yield
                for qb in range(2):
                    for i in range(NCH):
                        P.op("pe", MM(
                            psS[1][:, qb * 6:(qb + 1) * 6],
                            xovT[:, i * GQ + qb * 128: i * GQ + qb * 128 + 128],
                            rwtb[:, i * 6:(i + 1) * 6], i == 0, False),
                            reads=["rwtb", f"xovT{qb}"], writes=["psS1"])
                    P.op("pe", MM(psS[1][:, qb * 6:(qb + 1) * 6], ones1p[:],
                                  rbrow[:], False, True),
                         reads=["ones1p", "rbrow"], writes=["psS1"])
                P.op("dve", lambda e: e.tensor_copy(
                    out=lg_sb[:], in_=psS[1][:, 0:12]),
                    reads=["psS1"], writes=["lg_sb"])

                yield
                P.op("dve", lambda e: e.reduce_max(
                    out=sm[:, 16:18],
                    in_=lg_sb[:, 0:12].rearrange("p (q s) -> p q s", q=2),
                    axis=AX.X),
                    reads=["lg_sb"], writes=["sm_mx"])
                P.op("dve", lambda e: e.tensor_scalar_mul(
                    out=sm[:, 18:20], in0=sm[:, 16:18], scalar1=-1.0),
                    reads=["sm_mx"], writes=["sm_nmx"])
                for qb in range(2):
                    P.op("act", lambda e, qb=qb: e.activation(
                        out=esb[:, qb * 6:(qb + 1) * 6],
                        in_=lg_sb[:, qb * 6:(qb + 1) * 6], func=AF.Exp,
                        bias=sm[:, 18 + qb:19 + qb]),
                        reads=["lg_sb", "sm_nmx"], writes=[f"esb{qb}"])
                P.op("dve", lambda e: e.reduce_sum(
                    out=sm[:, 20:22],
                    in_=esb[:, 0:12].rearrange("p (q s) -> p q s", q=2),
                    axis=AX.X),
                    reads=["esb0", "esb1"], writes=["sm_S"])
                P.op("dve", lambda e: e.tensor_scalar(
                    out=msk12[:], in0=esb[:, 0:12], scalar1=1.0,
                    scalar2=-1e30, op0=ALU.is_ge, op1=ALU.mult),
                    reads=["esb0", "esb1"], writes=["mh0", "mh1"])
                P.op("dve", lambda e: e.tensor_add(
                    out=msk12[:], in0=msk12[:], in1=esb[:, 0:12]),
                    reads=["mh0", "mh1", "esb0", "esb1"],
                    writes=["mh0", "mh1"])
                P.op("dve", lambda e: e.reduce_max(
                    out=sm[:, 22:24],
                    in_=msk12[:].rearrange("p (q s) -> p q s", q=2),
                    axis=AX.X),
                    reads=["mh0", "mh1"], writes=["sm_m2"])
                P.op("dve", lambda e: e.tensor_scalar(
                    out=sm[:, 24:26], in0=sm[:, 20:22], scalar1=1e-6,
                    scalar2=1.0, op0=ALU.mult, op1=ALU.add),
                    reads=["sm_S"], writes=["sm_den"])
                P.op("dve", lambda e: e.tensor_add(
                    out=sm[:, 24:26], in0=sm[:, 24:26], in1=sm[:, 22:24]),
                    reads=["sm_den", "sm_m2"], writes=["sm_den"])
                P.op("dve", lambda e: e.reciprocal(
                    out=sm[:, 26:28], in_=sm[:, 24:26]),
                    reads=["sm_den"], writes=["sm_rd"])
                for qb in range(2):
                    P.op("dve", lambda e, qb=qb: e.tensor_scalar(
                        out=msk12[:, qb * 6:(qb + 1) * 6],
                        in0=esb[:, qb * 6:(qb + 1) * 6],
                        scalar1=sm[:, 22 + qb:23 + qb],
                        scalar2=sm[:, 26 + qb:27 + qb],
                        op0=ALU.is_ge, op1=ALU.mult),
                        reads=[f"esb{qb}", "sm_m2", "sm_rd"],
                        writes=[f"mh{qb}"])
                    P.op("dve", lambda e, qb=qb: e.tensor_mul(
                        out=wtm[:, qb * 6:(qb + 1) * 6],
                        in0=msk12[:, qb * 6:(qb + 1) * 6],
                        in1=esb[:, qb * 6:(qb + 1) * 6]),
                        reads=[f"mh{qb}", f"esb{qb}"], writes=["wtm"])

                yield
                yield
                for qb in range(2):
                    P.op("pe", lambda t, qb=qb: t.transpose(
                        out=psT[0:6, qb * 128: qb * 128 + 128],
                        in_=wtm[:, qb * 6:(qb + 1) * 6], identity=identb[:]),
                        reads=["wtm", "identb"], writes=["psT"])
                for qb in range(2):
                    P.op("dve", lambda e, qb=qb: e.tensor_copy(
                        out=wfm[0:6, qb * 128: qb * 128 + 128],
                        in_=psT[0:6, qb * 128: qb * 128 + 128]),
                        reads=["psT"], writes=["wfm"])

                yield
                P.op("pe", MM(psS[0][0:96, 0:256], repb[:], wfm[:], True, True),
                     reads=["repb", "wfm"], writes=["psS0"])
                P.op("dve", lambda e: e.tensor_mul(
                    out=wh_sb[:], in0=psS[0][0:96, 0:256], in1=h96_sb[:]),
                    reads=["psS0", "h96_sb"], writes=["wh_sb"])

                yield
                for qb in range(2):
                    for h in range(2):
                        bidx = 2 * qb + h
                        sbank = "psS0" if bidx % 2 == 0 else "psS1"
                        bank = psS[bidx % 2]
                        for i in range(NCH):
                            P.op("pe", MM(
                                bank[:, 0:512],
                                xovT[:, i * GQ + qb * 128:
                                     i * GQ + qb * 128 + 128],
                                bwt[:, i * D + h * 512: i * D + (h + 1) * 512],
                                i == 0, False),
                                reads=[f"xovT{qb}", "bwt"], writes=[sbank])
                        P.op("pe", MM(bank[:, 0:512], ones1p[:],
                                      bbrow[0:1, h * 512:(h + 1) * 512],
                                      False, False),
                             reads=["ones1p", "bbrow"], writes=[sbank])
                        P.op("pe", MM(bank[:, 0:512],
                                      hs_sb[0:16, qb * 128: qb * 128 + 128],
                                      sbtb[0:16, h * 512:(h + 1) * 512],
                                      False, False),
                             reads=["hs_sb", "sbtb"], writes=[sbank])
                        P.op("pe", MM(bank[:, 0:512],
                                      wh_sb[0:96, qb * 128: qb * 128 + 128],
                                      rbtb[0:96, h * 512:(h + 1) * 512],
                                      False, True),
                             reads=["wh_sb", "rbtb"], writes=[sbank])
                        P.op("act", lambda e, qb=qb, h=h, bank=bank: e.activation(
                            out=astg[:, qb * D + h * 512: qb * D + (h + 1) * 512],
                            in_=bank[:, 0:512], func=AF.Copy),
                            reads=[sbank], writes=[f"astg{qb}"])
                        yield
                if dump:
                    P.op("pool", lambda e, g=g: e.dma_start(
                        out=dbg["dbg_f"][g], in_=astg[:]),
                        reads=["astg0", "astg1"], inc="dD_f", inc_by=16,
                        dma=True)

                yield
                for qb in range(2):
                    ln_tm(astg[:, qb * D:(qb + 1) * D], [f"astg{qb}"],
                          xov[:, qb * D:(qb + 1) * D], [f"xov{qb}"],
                          reg=qb)
                for qb in range(2):
                    P.op("dve", lambda e, qb=qb: e.tensor_add(
                        out=x1buf[:, qb * D:(qb + 1) * D],
                        in0=xov[:, qb * D:(qb + 1) * D],
                        in1=x1buf[:, qb * D:(qb + 1) * D]),
                        reads=[f"xov{qb}", f"x1buf{qb}"],
                        writes=[f"x1buf{qb}"])

                yield
                P.op("pool", lambda e, g=g: e.dma_start(
                    out=out[g].transpose([1, 0, 2]),
                    in_=x1buf[:].rearrange("p (s d) -> p s d", d=D)),
                    reads=["x1buf0", "x1buf1"], writes=[f"out{g}"],
                    inc="dOUT", inc_by=16, dma=True)

                yield

        pend = []

        def adv(n=1):
            for _ in range(n):
                while pend:
                    try:
                        next(pend[0][0])
                        break
                    except StopIteration:
                        pend.pop(0)

        def drain_epi():
            while pend:
                adv()

        for g in range(0 if p0_only else ng):
            gb = g % 2
            M = 4 * (g + 1)
            kbs = list(range(g + 1))
            if g % 2 == 1:
                kbs = kbs[::-1]
            slot_of = {}
            for kb in kbs[:RING - 1]:
                slot_of[kb] = tile_slot(kb, kbs)

            def plan_st(ms):
                kbi, sbi = ms // 4, ms % 4
                kb = kbs[kbi]
                if sbi == 0 and kbi + RING - 2 <= g:
                    k2 = kbs[kbi + RING - 2]
                    slot_of[k2] = tile_slot(k2, kbs[max(0, kbi - 1):])
                slot3 = slot_of[kb]
                m = mglob[0] + ms
                sbank = "psS0" if (m % 2 == 0) else "psS1"
                psSm = psS[m % 2]
                for i in range(NCH):
                    P.op("pe", MM(psSm[:, 0:256], kv_k(slot3, i, sbi),
                                  q_ap(gb, i), i == 0, i == NCH - 1),
                         reads=[f"kt{slot3}", f"qt{gb}"], writes=[sbank])
                P.op("act", lambda e, psSm=psSm, m=m: e.activation(
                    out=pt_ap(m), in_=psSm[:, 0:256], func=AF.Exp,
                    scale=SCALE, bias=cst[:, 1:2]),
                    reads=[sbank, "cst"], writes=[f"pt{m % PTRING}"])
                if kb == g:
                    P.op("dve", lambda e, m=m, sbi=sbi: e.tensor_mul(
                        out=pt_ap(m), in0=pt_ap(m),
                        in1=maskt[:, sbi * GQ:(sbi + 1) * GQ]),
                        reads=[f"pt{m % PTRING}", "maskt"],
                        writes=[f"pt{m % PTRING}"])

            def plan_av(ms):
                kbi, sbi = ms // 4, ms % 4
                kb = kbs[kbi]
                slot3 = slot_of[kb]
                m = mglob[0] + ms
                for qb in range(2):
                    for h in range(2):
                        P.op("pe", MM(
                            psBig[:, qb * D + h * 512:
                                  qb * D + (h + 1) * 512],
                            pt_ap(m, qb), kv_v(slot3, sbi, h),
                            ms == 0, ms == M - 1),
                            reads=[f"pt{m % PTRING}", f"vt{slot3}"],
                            writes=[BIGK[qb * 2 + h]])
                for qb in range(2):
                    col = ms * 2 + qb
                    P.op("pe", MM(psX[:, col:col + 1], pt_ap(m, qb),
                                  onescol[:], True, True),
                         reads=[f"pt{m % PTRING}", "onescol"],
                         writes=["psX"])

            LAG = 2
            for ms in range(M + LAG):
                if ms < M:
                    plan_st(ms)
                if ms >= LAG:
                    plan_av(ms - LAG)
                if ms % 2 == 0:
                    adv(1)
            mglob[0] += M
            while pend and pend[0][1] <= g - 2:
                try:
                    next(pend[0][0])
                except StopIteration:
                    pend.pop(0)
            ep = gb * 2
            P.op("dve", lambda e, M=M, ep=ep: e.reduce_sum(
                out=rsum[:, ep:ep + 2],
                in_=psX[:, 0:2 * M].rearrange("p (m q) -> p q m", q=2),
                axis=AX.X),
                reads=["psX"], writes=[f"rsum{gb}"])
            P.op("dve", lambda e, ep=ep: e.reciprocal(
                out=rsum[:, ep:ep + 2], in_=rsum[:, ep:ep + 2]),
                reads=[f"rsum{gb}"], writes=[f"rsum{gb}"])
            for qb in range(2):
                P.op("dve", lambda e, qb=qb, gb=gb, ep=ep:
                     e.scalar_tensor_tensor(
                    out=ovbf[:, gb * 2 * D + qb * D:
                             gb * 2 * D + (qb + 1) * D],
                    in0=psBig[:, qb * D:(qb + 1) * D],
                    scalar=rsum[:, ep + qb:ep + qb + 1],
                    in1=vown[:, gb * 2 * D + qb * D:
                             gb * 2 * D + (qb + 1) * D],
                    op0=ALU.mult, op1=ALU.add),
                    reads=[BIGK[qb * 2], BIGK[qb * 2 + 1], f"rsum{gb}",
                           f"vo{gb}"],
                    writes=[f"ovbf{gb}_{qb}"])
            if dump:
                P.op("pool", lambda e, g=g, gb=gb: e.dma_start(
                    out=dbg["dbg_ov"][g],
                    in_=ovbf[:, gb * 2 * D:(gb + 1) * 2 * D]),
                    reads=[f"ovbf{gb}_0", f"ovbf{gb}_1"], inc="dD_ov",
                    inc_by=16, dma=True)
            load_qv(g + 2)
            pend.append([epi_gen(g, gb, M), g])
        drain_epi()
        P.op("pool", lambda e: e.nop(),
             waits=[(s, P.cnt.get(s, 0)) for s in
                    ("dOUT", "dD_ov", "dD_xov", "dD_a", "dD_x1", "dD_f")])

        with nc.Block() as block:
            P.emit(block, sems)

    return nc


def build(ng=NG, n_slab_a=SEQ // KT, dump=False, p0_only=False):
    p1 = Plan(None)
    _plan(p1, ng, n_slab_a, dump, p0_only)
    p2 = Plan(p1.found)
    nc = _plan(p2, ng, n_slab_a, dump, p0_only)
    assert not (p2.found - p1.found), "scan missed deps"
    return nc



def _layouts(inputs, core):
    b, p = core // 2, core % 2
    x = np.asarray(inputs["L0_attn_x"], dtype=np.float32)
    xb = np.ascontiguousarray(x[b])
    rows = np.concatenate([
        np.arange(GQ) + GQ * (2 * j + p) for j in range(NG)])
    xq = np.ascontiguousarray(xb[rows])
    xkv = np.ascontiguousarray(
        xb.reshape(NG, 2, GQ, D)[:, [p, 1 - p]].reshape(SEQ, D))
    i256 = np.arange(GQ)
    mask = np.concatenate([
        (i256[:, None] <= i256[None, :]).astype(np.float32),
        np.full((GQ, GQ), float(p), dtype=np.float32)], axis=0)
    f32 = lambda a: np.ascontiguousarray(np.asarray(a, dtype=np.float32))
    def chunked(a):
        t = a.shape[0]
        return np.ascontiguousarray(
            a.T.reshape(NCH, 128, t // KT, KT).transpose(2, 1, 0, 3)
            .reshape(t // KT, 128, NCH * KT))
    return {
        "xkvC": chunked(xkv),
        "xqC": chunked(xq),
        "xq": xq,
        "wq": f32(inputs["L0_attn_Wq"]),
        "wk": f32(inputs["L0_attn_Wk"]),
        "wv": f32(inputs["L0_attn_Wv"]),
        "wfc": f32(inputs["L0_attn_Wfc"]),
        "bfc": f32(inputs["L0_attn_bfc"]).reshape(1, D),
        "gT": np.ascontiguousarray(
            f32(inputs["L0_attn_gamma"]).reshape(NCH, 128).T),
        "bT": np.ascontiguousarray(
            f32(inputs["L0_attn_beta"]).reshape(NCH, 128).T),
        "basewt": np.ascontiguousarray(f32(inputs["L0_ffn_base_W"]).T),
        "baseb": f32(inputs["L0_ffn_base_b"]).reshape(1, D),
        "sat": np.ascontiguousarray(f32(inputs["L0_ffn_shared_A"]).T),
        "sbt": np.ascontiguousarray(f32(inputs["L0_ffn_shared_B"]).T),
        "rat": np.ascontiguousarray(
            np.transpose(f32(inputs["L0_ffn_routed_A"]), (2, 0, 1)
                         ).reshape(D, 96)),
        "rbt": np.ascontiguousarray(
            np.transpose(f32(inputs["L0_ffn_routed_B"]), (0, 2, 1)
                         ).reshape(96, D)),
        "rwt": np.ascontiguousarray(f32(inputs["L0_ffn_router_W"]).T),
        "rb": f32(inputs["L0_ffn_router_b"]).reshape(1, 6),
        "rep": np.repeat(np.eye(6, dtype=np.float32), 16, axis=1),
        "ident": np.eye(128, dtype=np.float32),
        "maskT": mask,
    }


def kernel(**inputs):
    nc = build()
    in_maps = [_layouts(inputs, c) for c in range(NCORE)]
    res = run_bass_kernel_spmd(nc, in_maps, core_ids=list(range(NCORE)))
    outp = np.empty((NBATCH, SEQ, D), dtype=np.float32)
    for c in range(NCORE):
        b, p = c // 2, c % 2
        o = np.asarray(res.results[c]["out"]).reshape(NG, GQ, D)
        for j in range(NG):
            r0 = GQ * (2 * j + p)
            outp[b, r0:r0 + GQ] = o[j]
    return outp


# version 22 (final)
# speedup vs baseline: 1.0356x; 1.0043x over previous
import numpy as np
import concourse.bass as bass
import concourse.mybir as mybir
from concourse.bass_utils import run_bass_kernel_spmd
from contextlib import ExitStack

F32 = mybir.dt.float32
BF16 = mybir.dt.bfloat16
AF = mybir.ActivationFunctionType
ALU = mybir.AluOpType
AX = mybir.AxisListType

D = 1024
NCH = 8
SEQ = 8192
NBATCH = 4
NCORE = 8
GQ = 256
NG = 16
KT = 512
EPS = 1e-5
SCALE = 1.0 / 32.0
EXPB = -64.0
RING = 4
PTRING = 8
RACE_DETECT = False
TRUNC = None
FENCE_MODE = "full"

ENGINES = ("pe", "act", "dve", "sync", "pool")
ENG_SEM = {"pe": "pe", "act": "act", "dve": "dve", "sync": "msync",
           "pool": "mpool"}
DRAIN_ENGINES = {"act", "dve"}


class Plan:

    def __init__(self, needs=None):
        self.needs = needs
        self.found = set()
        self.ops = {e: [] for e in ENGINES}
        self.cnt = {}
        self.lastwait = {}
        self.seq = {e: 0 for e in ENGINES}
        self.handle = {}
        self.res = {}

    def op(self, eng, fn, reads=(), writes=(), inc=None, inc_by=1,
           waits=(), dma=False, drain=False):
        seq = self.seq[eng]
        self.seq[eng] += 1
        deps = list(waits)
        need_drain = drain

        def want(key):
            h = self.handle.get(key)
            if h is None:
                self.found.add(key)
            else:
                deps.append(h)

        for key in list(reads) + list(writes):
            st = self.res.get(key)
            if st is None:
                continue
            w = st["w"]
            if w is not None:
                if w[0] == eng:
                    if eng in DRAIN_ENGINES:
                        need_drain = True
                else:
                    want(w)
            if key in writes:
                for reng, rseq in st["r"].items():
                    if reng == eng:
                        if eng in DRAIN_ENGINES:
                            need_drain = True
                    else:
                        want((reng, rseq))
        if (self.needs is not None and inc is None
                and (eng, seq) in self.needs):
            inc = ENG_SEM[eng]
            if eng in DRAIN_ENGINES:
                need_drain = True
        val = None
        if inc is not None:
            self.cnt[inc] = self.cnt.get(inc, 0) + inc_by
            val = self.cnt[inc]
            self.handle[(eng, seq)] = (inc, val)
        else:
            self.handle[(eng, seq)] = None
        for key in reads:
            st = self.res.setdefault(key, {"w": None, "r": {}})
            st["r"][eng] = seq
        for key in writes:
            st = self.res.setdefault(key, {"w": None, "r": {}})
            st["w"] = (eng, seq)
            st["r"] = {}
        real = []
        for s, v in deps:
            if v is None or v <= 0:
                continue
            k = (eng, s)
            if self.lastwait.get(k, 0) >= v:
                continue
            self.lastwait[k] = v
            real.append((s, v))
        self.ops[eng].append((real, need_drain, fn, inc, inc_by))
        return val

    def fence(self, qwaits=None):
        qw = list(qwaits or ())
        if FENCE_MODE == "none":
            self.res = {}
            return
        for e in ("pe", "act", "dve"):
            if e == "dve":
                fn = lambda x: x.engine_nop()
                w = []
            elif e == "act":
                fn = lambda x: x.activation(out=self.junk[0:1, 0:1],
                                            in_=self.junk[0:1, 1:2],
                                            func=AF.Copy)
                w = ["junkt"]
            else:
                fn = lambda x: x.matmul(out=self.pejunk[0:1, 0:1],
                                        lhsT=self.peones[0:1, 0:1],
                                        rhs=self.peones[0:1, 0:1],
                                        start=True, stop=True)
                w = ["psX"]
            self.op(e, fn, writes=[f"_f1{e}"] + w,
                    reads=(["onescol"] if e == "pe" else []),
                    drain=(e in DRAIN_ENGINES))
        if FENCE_MODE == "f1":
            self.res = {}
            return
        for e in ENGINES:
            self.op(e, lambda x: x.nop(),
                    reads=["_f1pe", "_f1act", "_f1dve"], waits=qw)
        self.res = {}

    def emit(self, block, sems):
        def mk(name):
            def run(eng):
                for waits, drain, fn, inc, inc_by in self.ops[name]:
                    for s, v in waits:
                        eng.wait_ge(sems[s], v)
                    if drain:
                        eng.drain()
                    inst = fn(eng)
                    if inc is not None:
                        inst.then_inc(sems[inc], inc_by)
            return run
        block.tensor(mk("pe"))
        block.scalar(mk("act"))
        block.vector(mk("dve"))
        block.sync(mk("sync"))
        block.gpsimd(mk("pool"))


SEM_NAMES = (["pe", "act", "dve", "msync", "mpool", "dW", "dOUT"]
             + [f"q_kt{i}" for i in range(RING)]
             + [f"q_vt{i}" for i in range(RING)]
             + ["q_qt0", "q_qt1", "q_vo0", "q_vo1", "q_x0g0", "q_x0g1",
                "q_xs0", "q_xs1", "q_xsbf0", "q_xsbf1",
                "q_kstg", "q_vstg",
                "dD_ov", "dD_xov", "dD_a", "dD_x1", "dD_f"])


def _plan(P, ng, n_slab_a, dump, p0_only=False):
    nc = bass.Bass("TRN2", target_bir_lowering=False,
               detect_race_conditions=RACE_DETECT)
    scrkind = "ExternalOutput" if dump else "Internal"
    n_slab_b = (ng * GQ) // KT

    dr = {}
    def din(name, shape, dt=F32):
        dr[name] = nc.dram_tensor(name, shape, dt, kind="ExternalInput").ap()
    din("xkvC", [SEQ // KT, 128, NCH * KT])
    din("xqC", [NG * GQ // KT, 128, NCH * KT])
    din("xq", [NG * GQ, D])
    din("wq", [D, D]); din("wk", [D, D]); din("wv", [D, D]); din("wfc", [D, D])
    din("bfc", [1, D])
    din("gT", [128, NCH])
    din("bT", [128, NCH])
    din("basewt", [D, D])
    din("baseb", [1, D])
    din("sat", [D, 16])
    din("sbt", [16, D])
    din("rat", [D, 96])
    din("rbt", [96, D])
    din("rwt", [D, 6])
    din("rb", [1, 6])
    din("rep", [6, 96])
    din("ident", [128, 128])
    din("maskT", [KT, GQ])

    out = nc.dram_tensor("out", [NG, 2, 128, D], F32,
                         kind="ExternalOutput").ap()
    dbg = {}
    if dump:
        for nm in ("dbg_ov", "dbg_x1", "dbg_f", "dbg_xov", "dbg_a"):
            dbg[nm] = nc.dram_tensor(nm, [NG, 128, 2 * D], F32,
                                     kind="ExternalOutput").ap()

    kscr = nc.dram_tensor("kscr", [SEQ // KT, 128, NCH * KT], BF16,
                          kind=scrkind).ap()
    vscr = nc.dram_tensor("vscr", [SEQ // KT, 128, 4 * D], BF16,
                          kind=scrkind).ap()
    qscr = nc.dram_tensor("qscr", [NG, 128, NCH * GQ], BF16,
                          kind=scrkind).ap()
    with ExitStack() as st:
        def sb(name, shape, dt):
            return st.enter_context(nc.sbuf_tensor("s_" + name, shape, dt))
        def psal(name, shape, dt=F32):
            return st.enter_context(nc.psum_tensor(name, shape, dt))

        psS = [psal("psS0", [128, 512]), psal("psS1", [128, 512])]
        psBig = psal("psBig", [128, 2048])
        psX = psal("psX", [128, 512])
        psT = psal("psT", [128, 1024], BF16)
        BIGK = ["psB0", "psB1", "psB2", "psB3"]

        wfc = sb("wfc", [128, NCH * D], BF16)
        bwt = sb("bwt", [128, NCH * D], BF16)
        satb = sb("satb", [128, NCH * 16], BF16)
        sbtb = sb("sbtb", [16, D], BF16)
        ratb = sb("ratb", [128, NCH * 96], BF16)
        rbtb = sb("rbtb", [96, D], BF16)
        rwtb = sb("rwtb", [128, NCH * 6], BF16)
        rbrow = sb("rbrow", [1, 6], BF16)
        repb = sb("repb", [6, 96], BF16)
        identb = sb("identb", [128, 128], BF16)
        maskt = sb("maskt", [128, 4 * GQ], BF16)
        gamT = sb("gamT", [128, NCH], F32)
        betT = sb("betT", [128, NCH], F32)
        bt8 = sb("bt8", [128, NCH], BF16)
        bfcrow = sb("bfcrow", [1, D], F32)
        bbrow = sb("bbrow", [1, D], BF16)
        ccbf = sb("ccbf", [1, D], BF16)
        onesf = sb("onesf", [1, 128], F32)
        ones1p = sb("ones1p", [1, 128], BF16)
        onescol = sb("onescol", [128, 1], BF16)
        cst = sb("cst", [128, 2], F32)
        junk = sb("junk", [1, 2], F32)
        ccb_bc = sb("ccb_bc", [128, D], BF16)
        bb_bc = sb("bb_bc", [128, D], BF16)
        P.junk = junk
        P.peones = onescol
        P.pejunk = psX

        sems = {n: st.enter_context(nc.semaphore(n)) for n in SEM_NAMES}

        def MM(o, l, r, start, stop):
            return lambda t: t.matmul(out=o, lhsT=l, rhs=r,
                                      start=start, stop=stop)

        P.op("dve", lambda e: e.memset(onesf[:], 1.0), writes=["onesf"])
        P.op("dve", lambda e: e.memset(ones1p[:], 1.0), writes=["ones1p"])
        P.op("dve", lambda e: e.memset(onescol[:], 1.0), writes=["onescol"])
        P.op("dve", lambda e: e.memset(cst[:, 0:1], EPS), writes=["cst"])
        P.op("dve", lambda e: e.memset(junk[:], 0.0), writes=["junkt"])
        P.op("dve", lambda e: e.memset(cst[:, 1:2], EXPB), writes=["cst"])

        def WL(dst, src, wkey):
            P.op("pool", lambda e, d=dst, s=src: e.dma_start(out=d, in_=s),
                 writes=[wkey], inc="dW", inc_by=16, dma=True)

        def rr(a):
            return a.rearrange("(c p) o -> p c o", p=128)

        def chv(t, x):
            return t[:].rearrange("p (c o) -> p c o", o=x)

        epiw = [
            lambda: WL(chv(wfc, D), rr(dr["wfc"]), "wfc"),
            lambda: WL(chv(bwt, D), rr(dr["basewt"]), "bwt"),
            lambda: WL(chv(satb, 16), rr(dr["sat"]), "satb"),
            lambda: WL(sbtb[:], dr["sbt"], "sbtb"),
            lambda: WL(chv(ratb, 96), rr(dr["rat"]), "ratb"),
            lambda: WL(rbtb[:], dr["rbt"], "rbtb"),
            lambda: WL(chv(rwtb, 6), rr(dr["rwt"]), "rwtb"),
            lambda: WL(rbrow[:], dr["rb"], "rbrow"),
            lambda: WL(repb[:], dr["rep"], "repb"),
            lambda: WL(identb[:], dr["ident"], "identb"),
            lambda: WL(maskt[:].rearrange("p (s q) -> p s q", q=GQ),
                       dr["maskT"].rearrange("(s p) q -> p s q", p=128),
                       "maskt"),
            lambda: WL(gamT[:], dr["gT"], "gamT"),
            lambda: WL(betT[:], dr["bT"], "betT"),
            lambda: WL(bfcrow[:], dr["bfc"], "bfcrow"),
            lambda: WL(bbrow[:], dr["baseb"], "bbrow"),
        ]

        def plan_epi_fence():
            P.op("pool", lambda e: e.nop(),
                 waits=[("dW", P.cnt["dW"])],
                 writes=["wfc", "bwt", "satb", "sbtb", "ratb", "rbtb",
                         "rwtb", "rbrow", "repb", "identb", "maskt", "gamT",
                         "betT", "bfcrow", "bbrow"])

        def plan_cconst():
            P.op("dve", lambda e: e.tensor_copy(out=bt8[:], in_=betT[:]),
                 reads=["betT"], writes=["bt8"])
            for h in range(2):
                for i in range(NCH):
                    P.op("pe", MM(
                        psX[0:1, 0:512], bt8[:, i:i + 1],
                        wfc[:, i * D + h * 512: i * D + (h + 1) * 512],
                        i == 0, i == NCH - 1),
                        reads=["bt8", "wfc"], writes=["psX"])
                P.op("dve", lambda e, h=h: e.tensor_add(
                    out=ccbf[0:1, h * 512:(h + 1) * 512],
                    in0=psX[0:1, 0:512],
                    in1=bfcrow[0:1, h * 512:(h + 1) * 512]),
                    reads=["psX", "bfcrow"], writes=["ccbf"])
            for h in range(2):
                P.op("pe", MM(psX[:, 0:512], ones1p[:],
                              ccbf[0:1, h * 512:(h + 1) * 512], True, True),
                     reads=["ones1p", "ccbf"], writes=["psX"])
                P.op("dve", lambda e, h=h: e.tensor_copy(
                    out=ccb_bc[:, h * 512:(h + 1) * 512], in_=psX[:, 0:512]),
                    reads=["psX"], writes=["ccb_bc"])
                P.op("pe", MM(psX[:, 0:512], ones1p[:],
                              bbrow[0:1, h * 512:(h + 1) * 512], True, True),
                     reads=["ones1p", "bbrow"], writes=["psX"])
                P.op("dve", lambda e, h=h: e.tensor_copy(
                    out=bb_bc[:, h * 512:(h + 1) * 512], in_=psX[:, 0:512]),
                    reads=["psX"], writes=["bb_bc"])
            for i in range(NCH):
                P.op("dve", lambda e, i=i: e.tensor_scalar_mul(
                    out=wfc[:, i * D:(i + 1) * D],
                    in0=wfc[:, i * D:(i + 1) * D],
                    scalar1=gamT[:, i:i + 1]),
                    reads=["wfc", "gamT"], writes=["wfc"])

        with ExitStack() as st0:
            def sb0(name, shape, dt):
                return st0.enter_context(
                    nc.sbuf_tensor("s_" + name, shape, dt))
            wq = sb0("wq", [128, NCH * D], BF16)
            wk = sb0("wk", [128, NCH * D], BF16)
            wv = sb0("wv", [128, NCH * D], BF16)
            cwq = sb0("cwq", [1, D], BF16)
            cwk = sb0("cwk", [1, D], BF16)
            cwv = sb0("cwv", [1, D], BF16)
            xs32 = sb0("xs32", [128, 2 * NCH * 512], F32)
            xsbf = sb0("xsbf", [128, NCH * 512], BF16)
            xsq = sb0("xsq", [128, NCH * 512], BF16)
            xhat = sb0("xhat", [128, 2 * NCH * 512], BF16)
            kstg = sb0("kstg", [128, NCH * 512], BF16)
            vstg = sb0("vstg", [128, 4 * D], BF16)
            r_mu = sb0("r_mu", [1, 512], F32)
            r_e2 = sb0("r_e2", [1, 512], F32)
            r_mu2 = sb0("r_mu2", [1, 512], F32)
            r_var = sb0("r_var", [1, 512], F32)
            r_r = sb0("r_r", [1, 512], F32)
            r_mr = sb0("r_mr", [1, 512], F32)
            r_mrb = sb0("r_mrb", [1, 1024], BF16)
            hold_r = sb0("hold_r", [1, 512], F32)
            hold_m = sb0("hold_m", [1, 512], BF16)

            def plan_cw(wten, crow, wkey, ckey):
                for h in range(2):
                    for i in range(NCH):
                        P.op("pe", MM(
                            psX[0:1, 0:512], onescol[:],
                            wten[:, i * D + h * 512: i * D + (h + 1) * 512],
                            i == 0, i == NCH - 1),
                            reads=["onescol", wkey], writes=["psX"])
                    P.op("dve", lambda e, crow=crow, h=h: e.tensor_scalar_mul(
                        out=crow[0:1, h * 512:(h + 1) * 512],
                        in0=psX[0:1, 0:512], scalar1=-1.0),
                        reads=["psX"], writes=[ckey])

            def p0_dma_xs(sl):
                srcC, tok_off, kdst, vdst, is_q, kdkeys, vdkeys, idx = sl
                buf = idx % 2
                xo = buf * NCH * 512
                src_ = srcC[tok_off // KT]
                P.op("sync", lambda e: e.dma_start(
                    out=xs32[:, xo:xo + NCH * 512], in_=src_),
                    writes=[f"xs{buf}"], inc=f"q_xs{buf}", inc_by=16,
                    dma=True)

            def p0_dma_xsbf(sl):
                srcC, tok_off, kdst, vdst, is_q, kdkeys, vdkeys, idx = sl
                if is_q:
                    return
                buf = idx % 2
                src_ = srcC[tok_off // KT]
                P.op("pool", lambda e: e.dma_start(
                    out=xsbf[:], in_=src_),
                    writes=["xsbf"], inc=f"q_xsbf{buf}", inc_by=16,
                    dma=True)

            def p0_stats_mm(sl):
                srcT, tok_off, kdst, vdst, is_q, kdkeys, vdkeys, idx = sl
                if is_q:
                    return
                buf = idx % 2
                xo = buf * NCH * 512
                xq_ = xsq
                P.op("act", lambda e: e.activation(
                    out=xq_[:], in_=xs32[:, xo:xo + NCH * 512],
                    func=AF.Square), reads=[f"xs{buf}"],
                    writes=["xsq"])
                for i in range(NCH):
                    P.op("pe", MM(psX[0:1, 0:512], onescol[:],
                                  xsbf[:, i * 512:(i + 1) * 512],
                                  i == 0, i == NCH - 1),
                         reads=["onescol", "xsbf"], writes=["psX"])
                for i in range(NCH):
                    P.op("pe", MM(psS[0][0:1, 0:512], onescol[:],
                                  xq_[:, i * 512:(i + 1) * 512],
                                  i == 0, i == NCH - 1),
                         reads=["onescol", "xsq"], writes=["psS0"])

            def p0_rows(sl):
                srcT, tok_off, kdst, vdst, is_q, kdkeys, vdkeys, idx = sl
                buf = idx % 2
                if is_q:
                    rmrb_ = r_mrb[0:1, buf * 512:(buf + 1) * 512]
                    P.op("dve", lambda e: e.tensor_copy(
                        out=r_r[0:1, 0:512], in_=hold_r[0:1, 0:512]),
                        reads=["hold_r0", "hold_r1"], writes=["r_r"])
                    P.op("dve", lambda e, rmrb_=rmrb_: e.tensor_copy(
                        out=rmrb_, in_=hold_m[0:1, 0:512]),
                        reads=["hold_m0", "hold_m1"], writes=[f"r_mrb{buf}"])
                    return
                rmu = r_mu[0:1, 0:512]
                rvv = r_var[0:1, 0:512]
                rrr = r_r[0:1, 0:512]
                rmrb = r_mrb[0:1, buf * 512:(buf + 1) * 512]
                P.op("dve", lambda e: e.tensor_scalar_mul(
                    out=rmu, in0=psX[0:1, 0:512], scalar1=1.0 / D),
                    reads=["psX"], writes=["r_mu"])
                P.op("dve", lambda e: e.tensor_scalar_mul(
                    out=r_e2[0:1, 0:512], in0=psS[0][0:1, 0:512],
                    scalar1=1.0 / D),
                    reads=["psS0"], writes=["r_e2"])
                P.op("act", lambda e: e.activation(
                    out=r_mu2[0:1, 0:512], in_=rmu, func=AF.Square),
                    reads=["r_mu"], writes=["r_mu2"])
                P.op("dve", lambda e: e.tensor_sub(
                    out=rvv, in0=r_e2[0:1, 0:512], in1=r_mu2[0:1, 0:512]),
                    reads=["r_e2", "r_mu2"], writes=["r_var"])
                P.op("act", lambda e: e.activation(
                    out=rvv, in_=rvv, func=AF.Ln, bias=cst[0:1, 0:1]),
                    reads=["r_var", "cst"], writes=["r_var"])
                P.op("act", lambda e: e.activation(
                    out=rrr, in_=rvv, func=AF.Exp, scale=-0.5),
                    reads=["r_var"], writes=["r_r"])
                P.op("dve", lambda e: e.tensor_mul(
                    out=r_mr[0:1, 0:512], in0=rmu, in1=rrr),
                    reads=["r_mu", "r_r"], writes=["r_mr"])
                P.op("dve", lambda e: e.tensor_copy(out=rmrb,
                                                    in_=r_mr[0:1, 0:512]),
                     reads=["r_mr"], writes=[f"r_mrb{buf}"])
                hp = (tok_off // KT) % 2
                P.op("dve", lambda e, hp=hp: e.tensor_copy(
                    out=hold_r[0:1, hp * 256:(hp + 1) * 256],
                    in_=rrr[0:1, 0:256]),
                    reads=["r_r"], writes=[f"hold_r{hp}"])
                P.op("dve", lambda e, hp=hp: e.tensor_copy(
                    out=hold_m[0:1, hp * 256:(hp + 1) * 256],
                    in_=r_mr[0:1, 0:256]),
                    reads=["r_mr"], writes=[f"hold_m{hp}"])

            def p0_xhat(sl):
                srcT, tok_off, kdst, vdst, is_q, kdkeys, vdkeys, idx = sl
                buf = idx % 2
                xo = buf * NCH * 512
                rrr = r_r[0:1, 0:512]
                P.op("pe", MM(psS[1][:, 0:512], onesf[:], rrr, True, True),
                     reads=["onesf", "r_r"], writes=["psS1"])
                for i in range(NCH):
                    P.op("dve", lambda e, i=i: e.tensor_mul(
                        out=xhat[:, buf * NCH * 512 + i * 512:
                                 buf * NCH * 512 + (i + 1) * 512],
                        in0=xs32[:, xo + i * 512: xo + (i + 1) * 512],
                        in1=psS[1][:, 0:512]),
                        reads=[f"xs{buf}", "psS1"],
                        writes=[f"xhat{buf}_{i}"])

            def p0_proj(sl):
                srcT, tok_off, kdst, vdst, is_q, kdkeys, vdkeys, idx = sl
                buf = idx % 2
                xh = lambda i: xhat[:, buf * NCH * 512 + i * 512:
                                    buf * NCH * 512 + (i + 1) * 512]
                rmrb = r_mrb[0:1, buf * 512:(buf + 1) * 512]
                wten = wq if is_q else wk
                wkey = "wq" if is_q else "wk"
                crow = cwq if is_q else cwk
                ckey = "cwq" if is_q else "cwk"
                for o in range(NCH):
                    bank = BIGK[o % 4]
                    for i in range(NCH):
                        P.op("pe", MM(
                            psBig[:, (o % 4) * 512:(o % 4 + 1) * 512],
                            wten[:, i * D + o * 128: i * D + (o + 1) * 128],
                            xh(i), i == 0, False),
                            reads=[wkey, f"xhat{buf}_{i}"], writes=[bank])
                    P.op("pe", MM(psBig[:, (o % 4) * 512:(o % 4 + 1) * 512],
                                  crow[0:1, o * 128:(o + 1) * 128], rmrb,
                                  False, True),
                         reads=[ckey, f"r_mrb{buf}"], writes=[bank])
                    P.op("dve", lambda e, o=o: e.tensor_copy(
                        out=kstg[:, o * 512:(o + 1) * 512],
                        in_=psBig[:, (o % 4) * 512:(o % 4 + 1) * 512]),
                        reads=[bank], writes=["kstg"])
                if is_q:
                    t2 = tok_off // KT
                    for u in range(2):
                        P.op("sync", lambda e, u=u, t2=t2: e.dma_start(
                            out=qscr[2 * t2 + u].rearrange(
                                "p (c q) -> p c q", q=GQ),
                            in_=kstg[:].rearrange(
                                "p (c t) -> p c t", t=512)[:, :, u * GQ:
                                                           (u + 1) * GQ]),
                            reads=["kstg"], writes=[kdkeys[u]],
                            inc="q_kstg", inc_by=16, dma=True)
                else:
                    P.op("sync", lambda e, kdst=kdst: e.dma_start(
                        out=kdst, in_=kstg[:]),
                        reads=["kstg"], writes=list(kdkeys),
                        inc="q_kstg", inc_by=16, dma=True)
                if is_q:
                    return
                for s4 in range(4):
                    for h in range(2):
                        bank = BIGK[(s4 * 2 + h) % 4]
                        bs = ((s4 * 2 + h) % 4) * 512
                        for i in range(NCH):
                            P.op("pe", MM(
                                psBig[:, bs:bs + 512],
                                xh(i)[:, s4 * 128: s4 * 128 + 128],
                                wv[:, i * D + h * 512: i * D + (h + 1) * 512],
                                i == 0, False),
                                reads=[f"xhat{buf}_{i}", "wv"],
                                writes=[bank])
                        P.op("pe", MM(psBig[:, bs:bs + 512],
                                      rmrb[0:1, s4 * 128: s4 * 128 + 128],
                                      cwv[0:1, h * 512:(h + 1) * 512],
                                      False, True),
                             reads=[f"r_mrb{buf}", "cwv"], writes=[bank])
                        P.op("dve", lambda e, s4=s4, h=h, bs=bs:
                             e.tensor_copy(
                            out=vstg[:, s4 * D + h * 512:
                                     s4 * D + (h + 1) * 512],
                            in_=psBig[:, bs:bs + 512]),
                            reads=[bank], writes=["vstg"])
                P.op("sync", lambda e, vdst=vdst: e.dma_start(
                    out=vdst, in_=vstg[:]),
                    reads=["vstg"], writes=list(vdkeys),
                    inc="q_vstg", inc_by=16, dma=True)

            order = []
            for s in range(n_slab_b):
                order += [("kv", 2 * s), ("kv", 2 * s + 1), ("q", s)]
            for t in range(2 * n_slab_b, n_slab_a):
                order.append(("kv", t))
            slabs = []
            for idx, (kind, t) in enumerate(order):
                if kind == "kv":
                    slabs.append((dr["xkvC"], t * KT, kscr[t], vscr[t],
                                  False, [f"kscr{t}"], [f"vscr{t}"], idx))
                else:
                    slabs.append((dr["xqC"], t * KT, None, None, True,
                                  [f"qscr{2 * t}", f"qscr{2 * t + 1}"],
                                  [], idx))
            nsl = len(slabs)
            if nsl:
                p0_dma_xs(slabs[0])
                p0_dma_xsbf(slabs[0])
                if nsl > 1:
                    p0_dma_xs(slabs[1])
                WL(chv(wk, D), rr(dr["wk"]), "wk")
                WL(chv(wv, D), rr(dr["wv"]), "wv")
                WL(chv(wq, D), rr(dr["wq"]), "wq")
                P.op("pool", lambda e: e.nop(),
                     waits=[("dW", P.cnt["dW"])], writes=["wq", "wk", "wv"])
                p0_stats_mm(slabs[0])
                p0_rows(slabs[0])
                plan_cw(wk, cwk, "wk", "cwk")
                p0_xhat(slabs[0])
                plan_cw(wv, cwv, "wv", "cwv")
                for t in range(nsl):
                    if t + 1 < nsl:
                        p0_dma_xsbf(slabs[t + 1])
                        p0_stats_mm(slabs[t + 1])
                        p0_rows(slabs[t + 1])
                    if t + 2 < nsl:
                        p0_dma_xs(slabs[t + 2])
                    p0_proj(slabs[t])
                    if t == 0:
                        plan_cw(wq, cwq, "wq", "cwq")
                    if t >= 1 and epiw:
                        epiw.pop(0)()
                        if len(epiw) >= 10:
                            epiw.pop(0)()
                    if t + 1 < nsl:
                        p0_xhat(slabs[t + 1])
                while epiw:
                    epiw.pop(0)()
                plan_epi_fence()
                plan_cconst()

        P.fence(qwaits=[(s, P.cnt.get(s, 0)) for s in
                        ("q_kstg", "q_vstg", "q_xsbf0", "q_xsbf1", "dW",
                         "q_xs0", "q_xs1")])

        ktile = sb("ktile", [128, RING * NCH * KT], BF16)
        vtile = sb("vtile", [128, RING * 4 * D], BF16)
        qtile = sb("qtile", [128, 2 * NCH * GQ], BF16)
        vown = sb("vown", [128, 2 * 2 * D], BF16)
        x0g = sb("x0g", [128, 2 * 2 * D], F32)
        ptb = sb("ptb", [128, PTRING * GQ], BF16)
        ovbf = sb("ovbf", [128, 2 * 2 * D], BF16)
        xov = sb("xov", [128, 2 * D], BF16)
        xovT = sb("xovT", [128, NCH * GQ], BF16)
        sqbuf = sb("sqbuf", [128, D], BF16)
        astg = sb("astg", [128, 2 * D], F32)
        x1buf = sb("x1buf", [128, 2 * D], F32)
        hs_sb = sb("hs_sb", [16, GQ], BF16)
        h96_sb = sb("h96_sb", [96, GQ], BF16)
        lg_sb = sb("lg_sb", [128, 12], F32)
        esb = sb("esb", [128, 12], F32)
        msk12 = sb("msk12", [128, 12], F32)
        wtm = sb("wtm", [128, 12], BF16)
        wfm = sb("wfm", [6, GQ], BF16)
        wh_sb = sb("wh_sb", [96, GQ], BF16)
        sm = sb("sm", [128, 64], F32)
        rsum = sb("rsum", [128, 4], F32)

        def kv_k(slot, i, sbi):
            off = slot * NCH * KT + i * KT + sbi * 128
            return ktile[:, off:off + 128]

        def kv_v(slot, sbi, h):
            off = slot * 4 * D + sbi * D + h * 512
            return vtile[:, off:off + 512]

        def q_ap(bufi, i):
            off = bufi * NCH * GQ + i * GQ
            return qtile[:, off:off + GQ]

        def pt_ap(m, qb=None):
            off = (m % PTRING) * GQ + (0 if qb is None else qb * 128)
            return ptb[:, off:off + (GQ if qb is None else 128)]

        mglob = [0]
        tcache = {}
        tclock = [0]

        def tile_slot(kb, future):
            for s, k in tcache.items():
                if k == kb:
                    return s
            if len(tcache) < RING:
                s = len(tcache)
            else:
                def nxt(k):
                    try:
                        return future.index(k)
                    except ValueError:
                        return 10 ** 9
                s = max(tcache, key=lambda sl: nxt(tcache[sl]))
            tcache[s] = kb
            P.op("sync", lambda e, kb=kb, s=s: e.dma_start(
                out=ktile[:, s * NCH * KT:(s + 1) * NCH * KT],
                in_=kscr[kb]),
                reads=[f"kscr{kb}"], writes=[f"kt{s}"],
                inc=f"q_kt{s}", inc_by=16, dma=True)
            P.op("sync", lambda e, kb=kb, s=s: e.dma_start(
                out=vtile[:, s * 4 * D:(s + 1) * 4 * D],
                in_=vscr[kb]),
                reads=[f"vscr{kb}"], writes=[f"vt{s}"],
                inc=f"q_vt{s}", inc_by=16, dma=True)
            return s

        def load_qv(g):
            if g >= ng:
                return
            gb = g % 2
            P.op("pool", lambda e, g=g, gb=gb: e.dma_start(
                out=qtile[:, gb * NCH * GQ:(gb + 1) * NCH * GQ],
                in_=qscr[g]),
                reads=[f"qscr{g}"], writes=[f"qt{gb}"],
                inc=f"q_qt{gb}", inc_by=16, dma=True)
            P.op("pool", lambda e, g=g, gb=gb: e.dma_start(
                out=vown[:, gb * 2 * D:(gb + 1) * 2 * D],
                in_=vscr[g][:, 0:2 * D]),
                reads=[f"vscr{g}"], writes=[f"vo{gb}"],
                inc=f"q_vo{gb}", inc_by=16, dma=True)

        def load_x0(g):
            if g >= ng:
                return
            gb = g % 2
            P.op("pool", lambda e, g=g, gb=gb: e.dma_start(
                out=x0g[:, gb * 2 * D:(gb + 1) * 2 * D].rearrange(
                    "p (s d) -> p s d", d=D),
                in_=dr["xq"][g * GQ:(g + 1) * GQ, :].rearrange(
                    "(s p) d -> p s d", p=128)),
                writes=[f"x0g{gb}"], inc=f"q_x0g{gb}", inc_by=16,
                dma=True)

        def ln_tm(src_ap, src_keys, dst_ap, dst_keys, reg=0):
            o = reg * 32
            P.op("dve", lambda e, o=o: e.bn_stats(
                out=sm[:, o:o + 6], in_=src_ap[:, 0:512]),
                reads=list(src_keys), writes=[f"sm_b0_{reg}"])
            P.op("dve", lambda e, o=o: e.bn_stats(
                out=sm[:, o + 6:o + 12], in_=src_ap[:, 512:1024]),
                reads=list(src_keys), writes=[f"sm_b1_{reg}"])
            P.op("dve", lambda e, o=o: e.bn_aggr(
                out=sm[:, o + 12:o + 14], in_=sm[:, o:o + 12]),
                reads=[f"sm_b0_{reg}", f"sm_b1_{reg}"],
                writes=[f"sm_mv_{reg}"])
            P.op("act", lambda e, o=o: e.activation(
                out=sm[:, o + 14:o + 15], in_=sm[:, o + 13:o + 14],
                func=AF.Ln, bias=cst[:, 0:1]),
                reads=[f"sm_mv_{reg}", "cst"], writes=[f"sm_ln_{reg}"])
            P.op("act", lambda e, o=o: e.activation(
                out=sm[:, o + 15:o + 16], in_=sm[:, o + 14:o + 15],
                func=AF.Exp, scale=-0.5),
                reads=[f"sm_ln_{reg}"], writes=[f"sm_r_{reg}"])
            P.op("dve", lambda e, o=o: e.tensor_scalar(
                out=dst_ap, in0=src_ap, scalar1=sm[:, o + 12:o + 13],
                scalar2=sm[:, o + 15:o + 16], op0=ALU.subtract,
                op1=ALU.mult),
                reads=list(src_keys) + [f"sm_mv_{reg}", f"sm_r_{reg}"],
                writes=list(dst_keys))

        def transpose8(src, src_keys, qb):
            for i in range(NCH):
                P.op("pe", lambda t, i=i: t.transpose(
                    out=psT[:, i * 128:(i + 1) * 128],
                    in_=src[:, i * 128:(i + 1) * 128], identity=identb[:]),
                    reads=list(src_keys) + ["identb"], writes=["psT"])
            for i in range(NCH):
                P.op("dve", lambda e, i=i, qb=qb: e.tensor_copy(
                    out=xovT[:, i * GQ + qb * 128: i * GQ + qb * 128 + 128],
                    in_=psT[:, i * 128:(i + 1) * 128]),
                    reads=["psT"], writes=[f"xovT{qb}"])

        load_qv(0)
        load_qv(1)
        load_x0(0)
        load_x0(1)

        def epi_gen(g, gb, M):
                for qb in range(2):
                    ln_tm(ovbf[:, gb * 2 * D + qb * D:
                               gb * 2 * D + (qb + 1) * D],
                          [f"ovbf{gb}_{qb}"],
                          xov[:, qb * D:(qb + 1) * D], [f"xov{qb}"],
                          reg=qb)
                    yield
                    transpose8(xov[:, qb * D:(qb + 1) * D], [f"xov{qb}"], qb)
                if dump:
                    P.op("pool", lambda e, g=g: e.dma_start(
                        out=dbg["dbg_xov"][g], in_=xov[:]),
                        reads=["xov0", "xov1"], inc="dD_xov", inc_by=16,
                        dma=True)

                yield
                for qb in range(2):
                    for h in range(2):
                        bidx = 2 * qb + h
                        sbank = "psS0" if bidx % 2 == 0 else "psS1"
                        bank = psS[bidx % 2]
                        for i in range(NCH):
                            P.op("pe", MM(
                                bank[:, 0:512],
                                xovT[:, i * GQ + qb * 128:
                                     i * GQ + qb * 128 + 128],
                                wfc[:, i * D + h * 512: i * D + (h + 1) * 512],
                                i == 0, False),
                                reads=[f"xovT{qb}", "wfc"], writes=[sbank])
                        P.op("pe", MM(bank[:, 0:512], ones1p[:],
                                      ccbf[0:1, h * 512:(h + 1) * 512],
                                      False, True),
                             reads=["ones1p", "ccbf"], writes=[sbank])
                        P.op("act", lambda e, qb=qb, h=h, bank=bank: e.activation(
                            out=astg[:, qb * D + h * 512: qb * D + (h + 1) * 512],
                            in_=bank[:, 0:512], func=AF.Copy),
                            reads=[sbank], writes=[f"astg{qb}"])
                        yield
                if dump:
                    P.op("pool", lambda e, g=g: e.dma_start(
                        out=dbg["dbg_a"][g], in_=astg[:]),
                        reads=["astg0", "astg1"], inc="dD_a", inc_by=16,
                        dma=True)

                yield
                for qb in range(2):
                    ln_tm(astg[:, qb * D:(qb + 1) * D], [f"astg{qb}"],
                          xov[:, qb * D:(qb + 1) * D], [f"xov{qb}"],
                          reg=qb)
                for qb in range(2):
                    P.op("dve", lambda e, qb=qb, gb=gb: e.tensor_add(
                        out=x1buf[:, qb * D:(qb + 1) * D],
                        in0=xov[:, qb * D:(qb + 1) * D],
                        in1=x0g[:, gb * 2 * D + qb * D:
                                gb * 2 * D + (qb + 1) * D]),
                        reads=[f"xov{qb}", f"x0g{gb}"],
                        writes=[f"x1buf{qb}"])
                load_x0(g + 2)
                if dump:
                    P.op("pool", lambda e, g=g: e.dma_start(
                        out=dbg["dbg_x1"][g], in_=x1buf[:]),
                        reads=["x1buf0", "x1buf1"], inc="dD_x1", inc_by=16,
                        dma=True)

                yield
                for qb in range(2):
                    ln_tm(x1buf[:, qb * D:(qb + 1) * D], [f"x1buf{qb}"],
                          xov[:, qb * D:(qb + 1) * D], [f"xov{qb}"],
                          reg=qb)
                    yield
                    transpose8(xov[:, qb * D:(qb + 1) * D], [f"xov{qb}"], qb)

                yield
                for i in range(NCH):
                    P.op("pe", MM(psS[0][0:96, 0:256],
                                  ratb[:, i * 96:(i + 1) * 96],
                                  xovT[:, i * GQ:(i + 1) * GQ],
                                  i == 0, i == NCH - 1),
                         reads=["ratb", "xovT0", "xovT1"], writes=["psS0"])
                for i in range(NCH):
                    P.op("pe", MM(psS[0][0:16, 256:512],
                                  satb[:, i * 16:(i + 1) * 16],
                                  xovT[:, i * GQ:(i + 1) * GQ],
                                  i == 0, i == NCH - 1),
                         reads=["satb", "xovT0", "xovT1"], writes=["psS0"])
                P.op("dve", lambda e: e.tensor_copy(
                    out=h96_sb[:], in_=psS[0][0:96, 0:256]),
                    reads=["psS0"], writes=["h96_sb"])
                P.op("dve", lambda e: e.tensor_copy(
                    out=hs_sb[:], in_=psS[0][0:16, 256:512]),
                    reads=["psS0"], writes=["hs_sb"])

                yield
                for qb in range(2):
                    for i in range(NCH):
                        P.op("pe", MM(
                            psS[1][:, qb * 6:(qb + 1) * 6],
                            xovT[:, i * GQ + qb * 128: i * GQ + qb * 128 + 128],
                            rwtb[:, i * 6:(i + 1) * 6], i == 0, False),
                            reads=["rwtb", f"xovT{qb}"], writes=["psS1"])
                    P.op("pe", MM(psS[1][:, qb * 6:(qb + 1) * 6], ones1p[:],
                                  rbrow[:], False, True),
                         reads=["ones1p", "rbrow"], writes=["psS1"])
                P.op("dve", lambda e: e.tensor_copy(
                    out=lg_sb[:], in_=psS[1][:, 0:12]),
                    reads=["psS1"], writes=["lg_sb"])

                yield
                P.op("dve", lambda e: e.reduce_max(
                    out=sm[:, 16:18],
                    in_=lg_sb[:, 0:12].rearrange("p (q s) -> p q s", q=2),
                    axis=AX.X),
                    reads=["lg_sb"], writes=["sm_mx"])
                P.op("dve", lambda e: e.tensor_scalar_mul(
                    out=sm[:, 18:20], in0=sm[:, 16:18], scalar1=-1.0),
                    reads=["sm_mx"], writes=["sm_nmx"])
                for qb in range(2):
                    P.op("act", lambda e, qb=qb: e.activation(
                        out=esb[:, qb * 6:(qb + 1) * 6],
                        in_=lg_sb[:, qb * 6:(qb + 1) * 6], func=AF.Exp,
                        bias=sm[:, 18 + qb:19 + qb]),
                        reads=["lg_sb", "sm_nmx"], writes=[f"esb{qb}"])
                P.op("dve", lambda e: e.reduce_sum(
                    out=sm[:, 20:22],
                    in_=esb[:, 0:12].rearrange("p (q s) -> p q s", q=2),
                    axis=AX.X),
                    reads=["esb0", "esb1"], writes=["sm_S"])
                P.op("dve", lambda e: e.tensor_scalar(
                    out=msk12[:], in0=esb[:, 0:12], scalar1=1.0,
                    scalar2=-1e30, op0=ALU.is_ge, op1=ALU.mult),
                    reads=["esb0", "esb1"], writes=["mh0", "mh1"])
                P.op("dve", lambda e: e.tensor_add(
                    out=msk12[:], in0=msk12[:], in1=esb[:, 0:12]),
                    reads=["mh0", "mh1", "esb0", "esb1"],
                    writes=["mh0", "mh1"])
                P.op("dve", lambda e: e.reduce_max(
                    out=sm[:, 22:24],
                    in_=msk12[:].rearrange("p (q s) -> p q s", q=2),
                    axis=AX.X),
                    reads=["mh0", "mh1"], writes=["sm_m2"])
                P.op("dve", lambda e: e.tensor_scalar(
                    out=sm[:, 24:26], in0=sm[:, 20:22], scalar1=1e-6,
                    scalar2=1.0, op0=ALU.mult, op1=ALU.add),
                    reads=["sm_S"], writes=["sm_den"])
                P.op("dve", lambda e: e.tensor_add(
                    out=sm[:, 24:26], in0=sm[:, 24:26], in1=sm[:, 22:24]),
                    reads=["sm_den", "sm_m2"], writes=["sm_den"])
                P.op("dve", lambda e: e.reciprocal(
                    out=sm[:, 26:28], in_=sm[:, 24:26]),
                    reads=["sm_den"], writes=["sm_rd"])
                for qb in range(2):
                    P.op("dve", lambda e, qb=qb: e.tensor_scalar(
                        out=msk12[:, qb * 6:(qb + 1) * 6],
                        in0=esb[:, qb * 6:(qb + 1) * 6],
                        scalar1=sm[:, 22 + qb:23 + qb],
                        scalar2=sm[:, 26 + qb:27 + qb],
                        op0=ALU.is_ge, op1=ALU.mult),
                        reads=[f"esb{qb}", "sm_m2", "sm_rd"],
                        writes=[f"mh{qb}"])
                    P.op("dve", lambda e, qb=qb: e.tensor_mul(
                        out=wtm[:, qb * 6:(qb + 1) * 6],
                        in0=msk12[:, qb * 6:(qb + 1) * 6],
                        in1=esb[:, qb * 6:(qb + 1) * 6]),
                        reads=[f"mh{qb}", f"esb{qb}"], writes=["wtm"])

                yield
                yield
                for qb in range(2):
                    P.op("pe", lambda t, qb=qb: t.transpose(
                        out=psT[0:6, qb * 128: qb * 128 + 128],
                        in_=wtm[:, qb * 6:(qb + 1) * 6], identity=identb[:]),
                        reads=["wtm", "identb"], writes=["psT"])
                for qb in range(2):
                    P.op("dve", lambda e, qb=qb: e.tensor_copy(
                        out=wfm[0:6, qb * 128: qb * 128 + 128],
                        in_=psT[0:6, qb * 128: qb * 128 + 128]),
                        reads=["psT"], writes=["wfm"])

                yield
                P.op("pe", MM(psS[0][0:96, 0:256], repb[:], wfm[:], True, True),
                     reads=["repb", "wfm"], writes=["psS0"])
                P.op("dve", lambda e: e.tensor_mul(
                    out=wh_sb[:], in0=psS[0][0:96, 0:256], in1=h96_sb[:]),
                    reads=["psS0", "h96_sb"], writes=["wh_sb"])

                yield
                for qb in range(2):
                    for h in range(2):
                        bidx = 2 * qb + h
                        sbank = "psS0" if bidx % 2 == 0 else "psS1"
                        bank = psS[bidx % 2]
                        for i in range(NCH):
                            P.op("pe", MM(
                                bank[:, 0:512],
                                xovT[:, i * GQ + qb * 128:
                                     i * GQ + qb * 128 + 128],
                                bwt[:, i * D + h * 512: i * D + (h + 1) * 512],
                                i == 0, False),
                                reads=[f"xovT{qb}", "bwt"], writes=[sbank])
                        P.op("pe", MM(bank[:, 0:512], ones1p[:],
                                      bbrow[0:1, h * 512:(h + 1) * 512],
                                      False, False),
                             reads=["ones1p", "bbrow"], writes=[sbank])
                        P.op("pe", MM(bank[:, 0:512],
                                      hs_sb[0:16, qb * 128: qb * 128 + 128],
                                      sbtb[0:16, h * 512:(h + 1) * 512],
                                      False, False),
                             reads=["hs_sb", "sbtb"], writes=[sbank])
                        P.op("pe", MM(bank[:, 0:512],
                                      wh_sb[0:96, qb * 128: qb * 128 + 128],
                                      rbtb[0:96, h * 512:(h + 1) * 512],
                                      False, True),
                             reads=["wh_sb", "rbtb"], writes=[sbank])
                        P.op("act", lambda e, qb=qb, h=h, bank=bank: e.activation(
                            out=astg[:, qb * D + h * 512: qb * D + (h + 1) * 512],
                            in_=bank[:, 0:512], func=AF.Copy),
                            reads=[sbank], writes=[f"astg{qb}"])
                        yield
                if dump:
                    P.op("pool", lambda e, g=g: e.dma_start(
                        out=dbg["dbg_f"][g], in_=astg[:]),
                        reads=["astg0", "astg1"], inc="dD_f", inc_by=16,
                        dma=True)

                yield
                for qb in range(2):
                    ln_tm(astg[:, qb * D:(qb + 1) * D], [f"astg{qb}"],
                          xov[:, qb * D:(qb + 1) * D], [f"xov{qb}"],
                          reg=qb)
                for qb in range(2):
                    P.op("dve", lambda e, qb=qb: e.tensor_add(
                        out=x1buf[:, qb * D:(qb + 1) * D],
                        in0=xov[:, qb * D:(qb + 1) * D],
                        in1=x1buf[:, qb * D:(qb + 1) * D]),
                        reads=[f"xov{qb}", f"x1buf{qb}"],
                        writes=[f"x1buf{qb}"])

                yield
                P.op("pool", lambda e, g=g: e.dma_start(
                    out=out[g].transpose([1, 0, 2]),
                    in_=x1buf[:].rearrange("p (s d) -> p s d", d=D)),
                    reads=["x1buf0", "x1buf1"], writes=[f"out{g}"],
                    inc="dOUT", inc_by=16, dma=True)

                yield

        pend = []

        def adv(n=1):
            for _ in range(n):
                while pend:
                    try:
                        next(pend[0][0])
                        break
                    except StopIteration:
                        pend.pop(0)

        def drain_epi():
            while pend:
                adv()

        for g in range(0 if p0_only else ng):
            gb = g % 2
            M = 4 * (g + 1)
            kbs = list(range(g + 1))
            if g % 2 == 1:
                kbs = kbs[::-1]
            slot_of = {}
            for kb in kbs[:RING - 1]:
                slot_of[kb] = tile_slot(kb, kbs)

            def plan_st(ms):
                kbi, sbi = ms // 4, ms % 4
                kb = kbs[kbi]
                if sbi == 0 and kbi + RING - 2 <= g:
                    k2 = kbs[kbi + RING - 2]
                    slot_of[k2] = tile_slot(k2, kbs[max(0, kbi - 1):])
                slot3 = slot_of[kb]
                m = mglob[0] + ms
                sbank = "psS0" if (m % 2 == 0) else "psS1"
                psSm = psS[m % 2]
                for i in range(NCH):
                    P.op("pe", MM(psSm[:, 0:256], kv_k(slot3, i, sbi),
                                  q_ap(gb, i), i == 0, i == NCH - 1),
                         reads=[f"kt{slot3}", f"qt{gb}"], writes=[sbank])
                P.op("act", lambda e, psSm=psSm, m=m: e.activation(
                    out=pt_ap(m), in_=psSm[:, 0:256], func=AF.Exp,
                    scale=SCALE, bias=cst[:, 1:2]),
                    reads=[sbank, "cst"], writes=[f"pt{m % PTRING}"])
                if kb == g:
                    P.op("dve", lambda e, m=m, sbi=sbi: e.tensor_mul(
                        out=pt_ap(m), in0=pt_ap(m),
                        in1=maskt[:, sbi * GQ:(sbi + 1) * GQ]),
                        reads=[f"pt{m % PTRING}", "maskt"],
                        writes=[f"pt{m % PTRING}"])

            def plan_av(ms):
                kbi, sbi = ms // 4, ms % 4
                kb = kbs[kbi]
                slot3 = slot_of[kb]
                m = mglob[0] + ms
                for qb in range(2):
                    for h in range(2):
                        P.op("pe", MM(
                            psBig[:, qb * D + h * 512:
                                  qb * D + (h + 1) * 512],
                            pt_ap(m, qb), kv_v(slot3, sbi, h),
                            ms == 0, ms == M - 1),
                            reads=[f"pt{m % PTRING}", f"vt{slot3}"],
                            writes=[BIGK[qb * 2 + h]])
                for qb in range(2):
                    col = ms * 2 + qb
                    P.op("pe", MM(psX[:, col:col + 1], pt_ap(m, qb),
                                  onescol[:], True, True),
                         reads=[f"pt{m % PTRING}", "onescol"],
                         writes=["psX"])

            LAG = 2
            for ms in range(M + LAG):
                if ms < M:
                    plan_st(ms)
                if ms >= LAG:
                    plan_av(ms - LAG)
                if ms % 2 == 0:
                    adv(1)
            mglob[0] += M
            while pend and pend[0][1] <= g - 2:
                try:
                    next(pend[0][0])
                except StopIteration:
                    pend.pop(0)
            ep = gb * 2
            P.op("dve", lambda e, M=M, ep=ep: e.reduce_sum(
                out=rsum[:, ep:ep + 2],
                in_=psX[:, 0:2 * M].rearrange("p (m q) -> p q m", q=2),
                axis=AX.X),
                reads=["psX"], writes=[f"rsum{gb}"])
            P.op("dve", lambda e, ep=ep: e.reciprocal(
                out=rsum[:, ep:ep + 2], in_=rsum[:, ep:ep + 2]),
                reads=[f"rsum{gb}"], writes=[f"rsum{gb}"])
            for qb in range(2):
                P.op("dve", lambda e, qb=qb, gb=gb, ep=ep:
                     e.scalar_tensor_tensor(
                    out=ovbf[:, gb * 2 * D + qb * D:
                             gb * 2 * D + (qb + 1) * D],
                    in0=psBig[:, qb * D:(qb + 1) * D],
                    scalar=rsum[:, ep + qb:ep + qb + 1],
                    in1=vown[:, gb * 2 * D + qb * D:
                             gb * 2 * D + (qb + 1) * D],
                    op0=ALU.mult, op1=ALU.add),
                    reads=[BIGK[qb * 2], BIGK[qb * 2 + 1], f"rsum{gb}",
                           f"vo{gb}"],
                    writes=[f"ovbf{gb}_{qb}"])
            if dump:
                P.op("pool", lambda e, g=g, gb=gb: e.dma_start(
                    out=dbg["dbg_ov"][g],
                    in_=ovbf[:, gb * 2 * D:(gb + 1) * 2 * D]),
                    reads=[f"ovbf{gb}_0", f"ovbf{gb}_1"], inc="dD_ov",
                    inc_by=16, dma=True)
            load_qv(g + 2)
            pend.append([epi_gen(g, gb, M), g])
        drain_epi()
        P.op("pool", lambda e: e.nop(),
             waits=[(s, P.cnt.get(s, 0)) for s in
                    ("dOUT", "dD_ov", "dD_xov", "dD_a", "dD_x1", "dD_f")])

        with nc.Block() as block:
            P.emit(block, sems)

    return nc


def build(ng=NG, n_slab_a=SEQ // KT, dump=False, p0_only=False):
    p1 = Plan(None)
    _plan(p1, ng, n_slab_a, dump, p0_only)
    p2 = Plan(p1.found)
    nc = _plan(p2, ng, n_slab_a, dump, p0_only)
    assert not (p2.found - p1.found), "scan missed deps"
    return nc



def _layouts(inputs, core):
    b, p = core // 2, core % 2
    x = np.asarray(inputs["L0_attn_x"], dtype=np.float32)
    xb = np.ascontiguousarray(x[b])
    rows = np.concatenate([
        np.arange(GQ) + GQ * (2 * j + p) for j in range(NG)])
    xq = np.ascontiguousarray(xb[rows])
    xkv = np.ascontiguousarray(
        xb.reshape(NG, 2, GQ, D)[:, [p, 1 - p]].reshape(SEQ, D))
    i256 = np.arange(GQ)
    mask = np.concatenate([
        (i256[:, None] <= i256[None, :]).astype(np.float32),
        np.full((GQ, GQ), float(p), dtype=np.float32)], axis=0)
    f32 = lambda a: np.ascontiguousarray(np.asarray(a, dtype=np.float32))
    def chunked(a):
        t = a.shape[0]
        return np.ascontiguousarray(
            a.T.reshape(NCH, 128, t // KT, KT).transpose(2, 1, 0, 3)
            .reshape(t // KT, 128, NCH * KT))
    return {
        "xkvC": chunked(xkv),
        "xqC": chunked(xq),
        "xq": xq,
        "wq": f32(inputs["L0_attn_Wq"]),
        "wk": f32(inputs["L0_attn_Wk"]),
        "wv": f32(inputs["L0_attn_Wv"]),
        "wfc": f32(inputs["L0_attn_Wfc"]),
        "bfc": f32(inputs["L0_attn_bfc"]).reshape(1, D),
        "gT": np.ascontiguousarray(
            f32(inputs["L0_attn_gamma"]).reshape(NCH, 128).T),
        "bT": np.ascontiguousarray(
            f32(inputs["L0_attn_beta"]).reshape(NCH, 128).T),
        "basewt": np.ascontiguousarray(f32(inputs["L0_ffn_base_W"]).T),
        "baseb": f32(inputs["L0_ffn_base_b"]).reshape(1, D),
        "sat": np.ascontiguousarray(f32(inputs["L0_ffn_shared_A"]).T),
        "sbt": np.ascontiguousarray(f32(inputs["L0_ffn_shared_B"]).T),
        "rat": np.ascontiguousarray(
            np.transpose(f32(inputs["L0_ffn_routed_A"]), (2, 0, 1)
                         ).reshape(D, 96)),
        "rbt": np.ascontiguousarray(
            np.transpose(f32(inputs["L0_ffn_routed_B"]), (0, 2, 1)
                         ).reshape(96, D)),
        "rwt": np.ascontiguousarray(f32(inputs["L0_ffn_router_W"]).T),
        "rb": f32(inputs["L0_ffn_router_b"]).reshape(1, 6),
        "rep": np.repeat(np.eye(6, dtype=np.float32), 16, axis=1),
        "ident": np.eye(128, dtype=np.float32),
        "maskT": mask,
    }


def kernel(**inputs):
    nc = build()
    in_maps = [_layouts(inputs, c) for c in range(NCORE)]
    res = run_bass_kernel_spmd(nc, in_maps, core_ids=list(range(NCORE)))
    outp = np.empty((NBATCH, SEQ, D), dtype=np.float32)
    for c in range(NCORE):
        b, p = c // 2, c % 2
        o = np.asarray(res.results[c]["out"]).reshape(NG, GQ, D)
        for j in range(NG):
            r0 = GQ * (2 * j + p)
            outp[b, r0:r0 + GQ] = o[j]
    return outp
